# version 01
# baseline (speedup 1.0000x reference)
import math
import numpy as np
import ml_dtypes
from contextlib import ExitStack

import concourse.bass as bass
import concourse.mybir as mybir
from concourse.bass_utils import run_bass_kernel_spmd

F32 = mybir.dt.float32
BF16 = mybir.dt.bfloat16
FP16 = mybir.dt.float16
FP8 = mybir.dt.float8e4
U16 = mybir.dt.uint16
AF = mybir.ActivationFunctionType
ALU = mybir.AluOpType

NCORE = 8
T = 4096
D = 1024
TT = 256
NT = T // TT
C = 128
NCH = TT // C
CONV = 2048
VAL = 1024
NPAIR = 8
H1 = 4096
PEN = 1.0e5
EPS_LN = 1e-5
EPS_RMS = 1e-6

N_TILES_EMIT = NT
DBG_SKIP = set()
DBG_PSTEP = 99
DBG_SCAN_ONLY = False
DBG_STOP_LN1 = False
DBG_STAGE = None
DBG_CHUNKS = None
DBG_PAIRS = None
DEBUG_TAPS = ()


class Op:
    __slots__ = ("eng", "fn", "need_inc", "sem", "amt", "waits", "idx_hint",
                 "cum", "note")

    def __init__(self, eng, fn, sem, amt, note=""):
        self.eng = eng
        self.fn = fn
        self.need_inc = False
        self.sem = sem
        self.amt = amt
        self.waits = {}
        self.note = note


class Sched:
    ENGINES = ("q", "g", "v", "s", "t")

    def __init__(self):
        self.ops = []
        self.res = {}
        self.dma_seq = {}
        self.eng_last = {}

    def _edge(self, prod, cons):
        if prod is None or prod is cons:
            return
        if (prod.eng == cons.eng and prod.eng not in ("q", "g")
                and self.eng_last.get(prod.eng) is not prod):
            return
        prod.need_inc = True
        cur = cons.waits.get(prod.sem)
        if cur is None or cur.idx_hint < prod.idx_hint:
            cons.waits[prod.sem] = prod

    def _track(self, o, reads, writes):
        o.idx_hint = len(self.ops)
        self.ops.append(o)
        for k in reads:
            st = self.res.setdefault(k, {"w": None, "r": []})
            self._edge(st["w"], o)
            if k.startswith("PB") or k.startswith("PQ"):
                for r in st["r"]:
                    if r.eng != o.eng:
                        self._edge(r, o)
            st["r"].append(o)
        for k in writes:
            st = self.res.setdefault(k, {"w": None, "r": []})
            self._edge(st["w"], o)
            for r in st["r"]:
                self._edge(r, o)
            st["w"] = o
            st["r"] = []
        self.eng_last[o.eng] = o
        return o

    def op(self, eng, fn, reads=(), writes=(), note=""):
        return self._track(Op(eng, fn, "sem_" + eng, 1, note), reads, writes)

    def dma(self, stream, fn, reads=(), writes=(), note="", queue="q"):
        prev = self.dma_seq.get(stream)
        o = Op(queue, fn, f"dma_{stream}", 16, note)
        o.need_inc = True
        self._track(o, reads, writes)
        if prev is not None:
            o.waits[o.sem] = prev
        self.dma_seq[stream] = o
        return o

    def finalize(self):
        counts = {}
        for o in self.ops:
            if o.need_inc:
                counts[o.sem] = counts.get(o.sem, 0) + o.amt
                o.cum = counts[o.sem]
        self.sem_totals = counts

    def emit(self, nc, ctx):
        sems = {}
        names = sorted({o.sem for o in self.ops if o.need_inc}
                       | {o2.sem for o in self.ops for o2 in o.waits.values()})
        for n in names:
            sems[n] = ctx.enter_context(nc.semaphore(n))
        per_eng = {e: [] for e in self.ENGINES}
        for o in self.ops:
            per_eng[o.eng].append(o)
        final_waits = [(sems[n], tot) for n, tot in sorted(self.sem_totals.items())
                       if n.startswith("dma_o_")]

        def mk_body(eng_name, add_drain):
            ops = per_eng[eng_name]

            def body(eng):
                hw = {}
                for o in ops:
                    for sem_name in sorted(o.waits):
                        v = o.waits[sem_name].cum
                        if hw.get(sem_name, 0) < v:
                            eng.wait_ge(sems[sem_name], v)
                            hw[sem_name] = v
                    ins = o.fn(eng)
                    if o.need_inc:
                        ins.then_inc(sems[o.sem], o.amt)
                if add_drain:
                    for s, tot in final_waits:
                        eng.wait_ge(s, tot)
            return body

        with nc.Block() as block:
            block.sync(mk_body("q", False))
            if per_eng["g"]:
                block.gpsimd(mk_body("g", False))
            block.vector(mk_body("v", False))
            block.scalar(mk_body("s", False))
            block.tensor(mk_body("t", True))


def build_program():
    nc = bass.Bass("TRN2", target_bir_lowering=False)
    sch = Sched()

    dr = {}
    dr["xT"] = nc.dram_tensor("xT", [D, T], F32, kind="ExternalInput").ap()
    dr["maskT"] = nc.dram_tensor("maskT", [D, T], F32, kind="ExternalInput").ap()
    dr["wqkvT"] = nc.dram_tensor("wqkvT", [D, CONV], F32, kind="ExternalInput").ap()
    dr["wzT"] = nc.dram_tensor("wzT", [D, VAL], F32, kind="ExternalInput").ap()
    dr["wbaT"] = nc.dram_tensor("wbaT", [D, 48], F32, kind="ExternalInput").ap()
    dr["woutT"] = nc.dram_tensor("woutT", [VAL, D], F32, kind="ExternalInput").ap()
    dr["w1T"] = nc.dram_tensor("w1T", [D, H1], F32, kind="ExternalInput").ap()
    dr["w2T"] = nc.dram_tensor("w2T", [H1, D], F32, kind="ExternalInput").ap()
    dr["convw"] = nc.dram_tensor("convw", [128, 64], F32, kind="ExternalInput").ap()
    dr["b1r"] = nc.dram_tensor("b1r", [128, 32], F32, kind="ExternalInput").ap()
    dr["b2r"] = nc.dram_tensor("b2r", [128, 8], F32, kind="ExternalInput").ap()
    dr["alog16"] = nc.dram_tensor("alog16", [16, 1], F32, kind="ExternalInput").ap()
    dr["dtb16"] = nc.dram_tensor("dtb16", [16, 1], F32, kind="ExternalInput").ap()
    dr["nwdup"] = nc.dram_tensor("nwdup", [128, 1], F32, kind="ExternalInput").ap()
    dr["outT"] = nc.dram_tensor("outT", [D, T], F32, kind="ExternalOutput").ap()
    taps = {}
    for name in DEBUG_TAPS:
        taps[name] = nc.dram_tensor("tap_" + name, [128, 8192], F32,
                                    kind="ExternalOutput").ap()

    def bf(x):
        return np.asarray(x, dtype=ml_dtypes.bfloat16).view(np.uint16)

    ji, ii = np.meshgrid(np.arange(128), np.arange(128), indexing="ij")
    cn = {}
    cn["id_b"] = nc.inline_tensor(bf(np.eye(128)), name="c0")
    cn["id_f"] = nc.inline_tensor(np.eye(128, dtype=np.float32), name="c1")
    cn["utones"] = nc.inline_tensor(bf((ji <= ii).astype(np.float32)), name="c2")
    cn["penL"] = nc.inline_tensor(bf(np.where(ii > ji, PEN, 0.0)), name="c3")
    cn["offdiag"] = nc.inline_tensor(bf(1.0 - np.eye(128)), name="c4")
    cn["onesrow"] = nc.inline_tensor(bf(np.ones((1, 128))), name="c5")
    cn["onescol"] = nc.inline_tensor(bf(np.ones((128, 1))), name="c6")
    hind = np.zeros((2, 128), np.float32)
    hind[0, 0:64] = 1.0
    hind[1, 64:128] = 1.0
    cn["halfind"] = nc.inline_tensor(bf(hind), name="c7")
    bh = np.zeros((128, 2), np.float32)
    bh[0:64, 0] = 1.0
    bh[64:128, 1] = 1.0
    cn["blockhalf"] = nc.inline_tensor(bf(bh), name="c8")
    hs = np.zeros((16, 16 * 128), np.float32)
    for h in range(16):
        hs[h, h * 128:(h + 1) * 128] = 1.0
    cn["hsel"] = nc.inline_tensor(bf(hs), name="c9")

    ctx = ExitStack()
    sb = lambda n, shp, dt: ctx.enter_context(nc.sbuf_tensor(n, shp, dt))
    ps = lambda n, shp, dt=F32: ctx.enter_context(nc.psum_tensor(n, shp, dt))

    tl = {}
    for k, shp, dt in [
        ("id_b", [128, 128], BF16), ("id_f", [128, 128], F32),
        ("utones", [128, 128], BF16), ("penL", [128, 128], BF16),
        ("offdiag", [128, 128], BF16), ("onesrow", [1, 128], BF16),
        ("onescol", [128, 1], BF16), ("halfind", [2, 128], BF16),
        ("blockhalf", [128, 2], BF16), ("hsel", [16, 16 * 128], BF16),
        ("convw", [128, 64], F32), ("b1r", [128, 32], F32), ("b2r", [128, 8], F32),
        ("alog16", [16, 1], F32), ("dtb16", [16, 1], F32), ("negA", [16, 1], F32),
        ("nwdup", [128, 1], F32), ("eps_ln", [1, 1], F32), ("eps_rms", [2, 1], F32),
        ("ln8", [16, 1], F32),
        ("wqkv", [128, 8 * CONV], BF16), ("wz", [128, 8 * VAL], BF16),
        ("wba", [128, 8 * 48], BF16), ("wout", [128, 8 * D], BF16),
        ("w1", [128, 8 * H1], FP8), ("w2", [128, 32 * D], FP8),
        ("b1q", [128, 32], F32), ("b2q", [128, 8], F32), ("bq8", [128, 32], FP8),
        ("x32", [128, 8 * TT], F32), ("xb", [128, 8 * TT], BF16),
        ("mixed", [128, 16 * (TT + 3)], BF16), ("zs", [128, 8 * TT], BF16),
        ("bat", [48, TT], F32), ("ga", [16, TT], F32), ("gsc", [16, TT], F32),
        ("cvt", [128, TT], F32), ("hb", [128, 8 * TT], BF16),
        ("xq", [128, 8 * TT], FP8), ("h1", [128, 32 * TT], FP8),
        ("f16t", [128, TT], FP16), ("ftmp", [128, TT], F32),
        ("msk", [128, 8 * TT], BF16),
        ("lnrow", [1, 2 * TT], F32), ("lnb", [1, 2 * TT], BF16),
        ("S", [128, NPAIR * 128], F32), ("Sb", [128, NPAIR * 128], BF16),
        ("btm", [128, 16], F32), ("gtmh", [128, 16], BF16), ("gtml", [128, 16], BF16),
        ("gtmp", [128, 16], F32), ("brows", [16, 128], F32), ("bsp", [16, 3 * 128], BF16),
        ("bsc", [16, 2 * 128], F32),
        ("betatm", [128, 16], F32), ("negebtm", [128, 16], F32),
        ("ebtm", [128, 16], F32), ("ebcmbfm", [16, 128], F32), ("ebcmbtm", [128, 16], F32),
        ("ebqrowb", [16, 128], BF16), ("bcrow", [1, 16], F32), ("ebcrow", [1, 16], F32),
        ("ebcrowb", [1, 2 * 16], BF16), ("ebc", [128, 16], F32),
        ("E", [128, 2 * 128], BF16), ("KKz", [128, 2 * 128], BF16), ("KQ", [128, 128], BF16),
        ("A", [128, 2 * 128], BF16), ("At", [128, 2 * 128], BF16),
        ("N0", [128, 2 * 128], BF16), ("Nt", [128, 2 * 128], BF16),
        ("sqA", [128, 4 * 128], BF16), ("sqB", [128, 4 * 128], BF16),
        ("Vtm", [128, 128], BF16), ("Ktm", [128, 64], BF16), ("Qt", [128, 2 * 128], BF16),
        ("rpre", [128, 128], F32), ("U", [128, 128], BF16),
        ("hpre", [128, 128], F32), ("hsq", [128, 128], BF16),
        ("istd", [2, 128], F32), ("istdb", [2, 128], BF16),
        ("stg", [128, 512], F32),
    ]:
        tl[k] = sb("t_" + k, shp, dt)

    pb = [ps(f"pb{i}", [128, 512], F32) for i in range(6)]
    pq = [ps(f"pq{i}", [128, 1024], BF16) for i in range(2)]
    PB = [f"PB{i}" for i in range(6)]
    PQ = [f"PQ{i}" for i in range(2)]

    def mm(out_ap, lhsT_ap, rhs_ap, start, stop, reads, writes, tp=None):
        def fn(eng, o=out_ap, l=lhsT_ap, r=rhs_ap, st=start, sp=stop, tpos=tp):
            return eng.matmul(o, l, r, start=st, stop=sp, tile_position=tpos)
        return sch.op("t", fn, reads=reads, writes=writes)

    def tpose(out_ap, in_ap, ident_ap, reads, writes):
        def fn(eng, o=out_ap, i=in_ap, d=ident_ap):
            return eng.transpose(o, i, d)
        return sch.op("t", fn, reads=reads, writes=writes)

    def vop(name, *args, reads=(), writes=()):
        def fn(eng, n=name, a=args):
            return getattr(eng, n)(*a)
        return sch.op("v", fn, reads=reads, writes=writes)

    def sact(out_ap, in_ap, func, bias, scale, reads, writes):
        def fn(eng, o=out_ap, i=in_ap, f=func, b=bias, s=scale):
            return eng.activation(o, i, f, bias=b, scale=s)
        return sch.op("s", fn, reads=reads, writes=writes)

    def scopy(out_ap, in_ap, reads, writes):
        def fn(eng, o=out_ap, i=in_ap):
            return eng.copy(o, i)
        return sch.op("s", fn, reads=reads, writes=writes)

    def dma(stream, out_ap, in_ap, reads, writes, queue="q"):
        def fn(eng, o=out_ap, i=in_ap):
            return eng.dma_start(out=o, in_=i)
        return sch.dma(stream, fn, reads=reads, writes=writes, queue=queue)

    def mxc(c, a, b2_):
        return tl["mixed"][:, c * (TT + 3) + a: c * (TT + 3) + b2_]

    for k in ("id_b", "utones", "penL", "offdiag", "onesrow", "onescol",
              "halfind", "blockhalf", "hsel"):
        dma("c", tl[k][:], cn[k].ap().bitcast(BF16), reads=(), writes=(k,))
    dma("c", tl["id_f"][:], cn["id_f"].ap(), reads=(), writes=("id_f",))
    for k in ("convw", "b1r", "b2r", "alog16", "dtb16", "nwdup"):
        dma("c", tl[k][:], dr[k], reads=(), writes=(k,))

    vop("memset", tl["eps_ln"][:], EPS_LN, writes=("eps_ln",))
    vop("memset", tl["eps_rms"][:], EPS_RMS, writes=("eps_rms",))
    vop("memset", tl["ln8"][:], math.log(0.125), writes=("ln8",))
    vop("memset", tl["S"][:], 0.0, writes=("S",))
    vop("memset", tl["Sb"][:], 0.0, writes=("Sb",))
    for c in range(16):
        vop("memset", mxc(c, 0, 3), 0.0, writes=("mixed",))
    sact(tl["negA"][:], tl["alog16"][:], AF.Exp, 0.0, 1.0,
         reads=("alog16",), writes=("negA",))
    vop("tensor_scalar_mul", tl["negA"][:], tl["negA"][:], -1.0,
        reads=("negA",), writes=("negA",))

    def stage_cast_big(dst_tile, dst_off, src_ap2d, wkey, cols):
        for half in range(0, cols, 512):
            w = min(512, cols - half)
            dma("w", tl["stg"][:, 0:w], src_ap2d[:, half:half + w],
                reads=("stg",), writes=("stg",))
            vop("tensor_copy", dst_tile[:, dst_off + half: dst_off + half + w],
                tl["stg"][:, 0:w], reads=("stg",), writes=(wkey,))

    for kc in range(8):
        stage_cast_big(tl["wqkv"], kc * CONV,
                       dr["wqkvT"][kc * 128:(kc + 1) * 128, :], "wqkv", CONV)
        stage_cast_big(tl["wz"], kc * VAL,
                       dr["wzT"][kc * 128:(kc + 1) * 128, :], "wz", VAL)
        stage_cast_big(tl["wba"], kc * 48,
                       dr["wbaT"][kc * 128:(kc + 1) * 128, :], "wba", 48)
        stage_cast_big(tl["wout"], kc * D,
                       dr["woutT"][kc * 128:(kc + 1) * 128, :], "wout", D)
        stage_cast_big(tl["w1"], kc * H1,
                       dr["w1T"][kc * 128:(kc + 1) * 128, :], "w1", H1)
        for sub in range(4):
            kc2 = kc * 4 + sub
            stage_cast_big(tl["w2"], kc2 * D,
                           dr["w2T"][kc2 * 128:(kc2 + 1) * 128, :], "w2", D)
    vop("tensor_copy", tl["bq8"][:, 0:32], tl["b1r"][:], reads=("b1r",), writes=("bq8",))
    vop("tensor_copy", tl["b1q"][:], tl["bq8"][:, 0:32], reads=("bq8",), writes=("b1q",))
    vop("tensor_copy", tl["bq8"][:, 0:8], tl["b2r"][:], reads=("b2r",), writes=("bq8",))
    vop("tensor_copy", tl["b2q"][:], tl["bq8"][:, 0:8], reads=("bq8",), writes=("b2q",))

    def emit_chunk(t, cc):
        col0 = cc * C
        idf16 = tl["id_f"][0:16, 0:16]
        tpose(pb[4][0:128, 0:16], tl["ga"][:, col0:col0 + C], idf16,
              reads=("ga", "id_f"), writes=(PB[4],))
        vop("tensor_copy", tl["gtmh"][:], pb[4][0:128, 0:16], reads=(PB[4],), writes=("gtmh",))
        vop("tensor_sub", tl["gtmp"][:], pb[4][0:128, 0:16], tl["gtmh"][:],
            reads=(PB[4], "gtmh"), writes=("gtmp",))
        vop("tensor_copy", tl["gtml"][:], tl["gtmp"][:], reads=("gtmp",), writes=("gtml",))
        mm(pb[5][0:128, 0:16], tl["utones"][:], tl["gtmh"][:], True, False,
           reads=("gtmh", "utones"), writes=(PB[5],))
        mm(pb[5][0:128, 0:16], tl["utones"][:], tl["gtml"][:], False, True,
           reads=("gtml", "utones"), writes=(PB[5],))
        vop("tensor_copy", tl["btm"][:], pb[5][0:128, 0:16], reads=(PB[5],), writes=("btm",))
        tpose(pb[4][0:16, 16:144], tl["btm"][:], tl["id_f"][:],
              reads=("btm", "id_f"), writes=(PB[4],))
        vop("tensor_copy", tl["brows"][:], pb[4][0:16, 16:144], reads=(PB[4],), writes=("brows",))
        vop("tensor_copy", tl["bsp"][:, 0:128], tl["brows"][:], reads=("brows",), writes=("bsp",))
        vop("tensor_sub", tl["bsc"][:, 0:128], tl["brows"][:], tl["bsp"][:, 0:128],
            reads=("brows", "bsp"), writes=("bsc",))
        vop("tensor_copy", tl["bsp"][:, 128:256], tl["bsc"][:, 0:128],
            reads=("bsc",), writes=("bsp",))
        vop("tensor_sub", tl["bsc"][:, 128:256], tl["bsc"][:, 0:128], tl["bsp"][:, 128:256],
            reads=("bsc", "bsp"), writes=("bsc",))
        vop("tensor_copy", tl["bsp"][:, 256:384], tl["bsc"][:, 128:256],
            reads=("bsc",), writes=("bsp",))
        tpose(pb[5][0:128, 16:32], tl["bat"][32:48, col0:col0 + C],
              tl["id_f"][32:48, 32:48], reads=("bat", "id_f"), writes=(PB[5],))
        sact(tl["betatm"][:], pb[5][0:128, 16:32], AF.Sigmoid, 0.0, 1.0,
             reads=(PB[5],), writes=("betatm",))
        sact(tl["ebtm"][:], tl["btm"][:], AF.Exp, 0.0, 1.0, reads=("btm",), writes=("ebtm",))
        vop("tensor_scalar_mul", tl["negebtm"][:], tl["ebtm"][:], -1.0,
            reads=("ebtm",), writes=("negebtm",))
        sact(tl["ebcmbfm"][:], tl["brows"][:], AF.Exp, tl["brows"][:, 127:128], -1.0,
             reads=("brows",), writes=("ebcmbfm",))
        tpose(pb[5][0:128, 32:48], tl["ebcmbfm"][:], idf16,
              reads=("ebcmbfm", "id_f"), writes=(PB[5],))
        vop("tensor_copy", tl["ebcmbtm"][:], pb[5][0:128, 32:48],
            reads=(PB[5],), writes=("ebcmbtm",))
        sact(tl["ebqrowb"][:], tl["brows"][:], AF.Exp, tl["ln8"][:], 1.0,
             reads=("brows", "ln8"), writes=("ebqrowb",))
        tpose(pb[5][0:1, 48:64], tl["brows"][:, 127:128], idf16,
              reads=("brows", "id_f"), writes=(PB[5],))
        sact(tl["ebcrow"][:], pb[5][0:1, 48:64], AF.Exp, 0.0, 1.0,
             reads=(PB[5],), writes=("ebcrow",))
        vop("tensor_copy", tl["ebcrowb"][:, 0:16], tl["ebcrow"][:],
            reads=("ebcrow",), writes=("ebcrowb",))
        vop("tensor_sub", tl["bcrow"][:], tl["ebcrow"][:], tl["ebcrowb"][:, 0:16],
            reads=("ebcrow", "ebcrowb"), writes=("bcrow",))
        vop("tensor_copy", tl["ebcrowb"][:, 16:32], tl["bcrow"][:],
            reads=("bcrow",), writes=("ebcrowb",))
        mm(pb[5][0:128, 64:80], tl["onesrow"][:], tl["ebcrowb"][:, 0:16], True, False,
           reads=("ebcrowb", "onesrow"), writes=(PB[5],))
        mm(pb[5][0:128, 64:80], tl["onesrow"][:], tl["ebcrowb"][:, 16:32], False, True,
           reads=("ebcrowb", "onesrow"), writes=(PB[5],))
        vop("tensor_copy", tl["ebc"][:], pb[5][0:128, 64:80], reads=(PB[5],), writes=("ebc",))

        for p in range(NPAIR if DBG_PAIRS is None else DBG_PAIRS):
            emit_pair(t, cc, p)

    def emit_pair(t, cc, p):
        col0 = cc * C
        b = 64 * (p % 2)
        qrows = mxc(p // 2, 3 + col0, 3 + col0 + C)[b:b + 64, :]
        krows = mxc(4 + p // 2, 3 + col0, 3 + col0 + C)[b:b + 64, :]
        vpair = mxc(8 + p, 3 + col0, 3 + col0 + C)
        pm = pb[4 + (p % 2)]
        PM = PB[4 + (p % 2)]
        pe = [pb[p % 2], pb[1 - p % 2]]
        PE_ = [PB[p % 2], PB[1 - p % 2]]
        psq = [pb[2], pb[3]]
        PSQ = [PB[2], PB[3]]

        if "kkmm" not in DBG_SKIP:
            mm(pm[:, 0:128], krows, krows, True, True, reads=("mixed",), writes=(PM,))
        if "kqmm" not in DBG_SKIP:
            mm(pm[:, 128:256], qrows, krows, True, True, reads=("mixed",), writes=(PM,))
        if "kkts" not in DBG_SKIP:
            for hh in range(2):
                h = 2 * p + hh
                vop("tensor_scalar", tl["KKz"][:, hh * 128:(hh + 1) * 128], pm[:, 0:128],
                    tl["betatm"][:, h:h + 1], -1.0, ALU.mult, ALU.mult,
                    reads=(PM, "betatm"), writes=("KKz",))
        if "kqact" not in DBG_SKIP:
            sact(tl["KQ"][:], pm[:, 128:256], AF.Identity, 0.0, 0.125,
                 reads=(PM,), writes=("KQ",))
        if "kktt" not in DBG_SKIP:
            for hh in range(2):
                vop("tensor_mul", tl["KKz"][:, hh * 128:(hh + 1) * 128],
                    tl["KKz"][:, hh * 128:(hh + 1) * 128], tl["offdiag"][:],
                    reads=("KKz", "offdiag"), writes=("KKz",))

        if DBG_PSTEP < 1:
            return
        for hh in range(2):
            h = 2 * p + hh
            bank = pe[hh]
            key = PE_[hh]
            for s_i in range(3):
                mm(bank[:, 0:128], tl["hsel"][:, h * 128:(h + 1) * 128],
                   tl["bsp"][:, s_i * 128:(s_i + 1) * 128], s_i == 0, False,
                   reads=("hsel", "bsp"), writes=(key,))
            mm(bank[:, 0:128], tl["id_b"][:], tl["penL"][:], False, True,
               reads=("id_b", "penL"), writes=(key,))
            sact(tl["E"][:, hh * 128:(hh + 1) * 128], bank[:, 0:128], AF.Exp,
                 tl["btm"][:, h:h + 1], -1.0, reads=(key, "btm"), writes=("E",))
        if DBG_PSTEP < 2:
            return
        for hh in range(2):
            vop("tensor_mul", tl["N0"][:, hh * 128:(hh + 1) * 128],
                tl["KKz"][:, hh * 128:(hh + 1) * 128], tl["E"][:, hh * 128:(hh + 1) * 128],
                reads=("KKz", "E"), writes=("N0",))
            vop("tensor_mul", tl["A"][:, hh * 128:(hh + 1) * 128],
                tl["KQ"][:], tl["E"][:, hh * 128:(hh + 1) * 128],
                reads=("KQ", "E"), writes=("A",))
        if DBG_PSTEP < 3:
            return
        for hh in range(2):
            q_i = hh
            tpose(pq[q_i][:, 0:128], tl["N0"][:, hh * 128:(hh + 1) * 128], tl["id_b"][:],
                  reads=("N0", "id_b"), writes=(PQ[q_i],))
            vop("tensor_copy", tl["Nt"][:, hh * 128:(hh + 1) * 128], pq[q_i][:, 0:128],
                reads=(PQ[q_i],), writes=("Nt",))
            tpose(pq[q_i][:, 128:256], tl["A"][:, hh * 128:(hh + 1) * 128], tl["id_b"][:],
                  reads=("A", "id_b"), writes=(PQ[q_i],))
            vop("tensor_copy", tl["At"][:, hh * 128:(hh + 1) * 128], pq[q_i][:, 128:256],
                reads=(PQ[q_i],), writes=("At",))

        if DBG_PSTEP < 4:
            return
        tpose(pq[0][:, 256:384], vpair, tl["id_b"][:], reads=("mixed", "id_b"), writes=(PQ[0],))
        vop("tensor_copy", tl["Vtm"][:], pq[0][:, 256:384], reads=(PQ[0],), writes=("Vtm",))
        tpose(pq[1][:, 256:320], krows, tl["id_b"][b:b + 64, b:b + 64],
              reads=("mixed", "id_b"), writes=(PQ[1],))
        vop("tensor_copy", tl["Ktm"][:], pq[1][:, 256:320], reads=(PQ[1],), writes=("Ktm",))

        if DBG_PSTEP < 5:
            return
        for hh in range(2):
            h = 2 * p + hh
            mm(pm[b:b + 64, 256 + hh * 128:256 + (hh + 1) * 128],
               tl["hsel"][:, h * 128:h * 128 + 64], tl["ebqrowb"][:], True, True,
               reads=("hsel", "ebqrowb"), writes=(PM,), tp=(0, b))
            vop("tensor_mul", tl["Qt"][b:b + 64, hh * 128:(hh + 1) * 128], qrows,
                pm[b:b + 64, 256 + hh * 128:256 + (hh + 1) * 128],
                reads=("mixed", PM), writes=("Qt",))

        if DBG_PSTEP < 6:
            return
        for hh in range(2):
            h = 2 * p + hh
            mm(pm[:, 0:64] if hh == 0 else pm[:, 64:128], krows,
               tl["Sb"][b:b + 64, p * 128 + hh * 64: p * 128 + (hh + 1) * 64],
               True, True, reads=("mixed", "Sb"), writes=(PM,))
        for hh in range(2):
            h = 2 * p + hh
            vop("scalar_tensor_tensor", tl["rpre"][:, hh * 64:(hh + 1) * 64],
                pm[:, hh * 64:(hh + 1) * 64], tl["negebtm"][:, h:h + 1],
                tl["Vtm"][:, hh * 64:(hh + 1) * 64], ALU.mult, ALU.add,
                reads=(PM, "negebtm", "Vtm"), writes=("rpre",))
            vop("tensor_scalar_mul", tl["U"][:, hh * 64:(hh + 1) * 64],
                tl["rpre"][:, hh * 64:(hh + 1) * 64], tl["betatm"][:, h:h + 1],
                reads=("rpre", "betatm"), writes=("U",))

        if DBG_PSTEP < 7:
            return
        def apply_level(nsrc_tile, nsrc_key, cols_h1, cols_h2, bankA, keyA):
            for hh in range(2):
                colr = cols_h1 if hh == 0 else cols_h2
                mm(bankA[:, hh * 64:(hh + 1) * 64],
                   nsrc_tile[:, colr], tl["U"][:, hh * 64:(hh + 1) * 64],
                   True, True, reads=(nsrc_key, "U"), writes=(keyA,))
            vop("tensor_add", tl["U"][:], tl["U"][:], bankA[:, 0:128],
                reads=("U", keyA), writes=("U",))

        apply_level(tl["Nt"], "Nt", slice(0, 128), slice(128, 256), pe[0], PE_[0])
        src_t, src_k = None, None
        for lv in range(1, 7):
            sq_t = tl["sqA"] if lv % 2 else tl["sqB"]
            sq_k = "sqA" if lv % 2 else "sqB"
            bank = psq[lv % 2]
            bkey = PSQ[lv % 2]
            if lv == 1:
                lo = [(tl["N0"], "N0", slice(0, 128)), (tl["N0"], "N0", slice(128, 256))]
                up = [(tl["Nt"], "Nt", slice(0, 128)), (tl["Nt"], "Nt", slice(128, 256))]
            else:
                pt = tl["sqA"] if (lv - 1) % 2 else tl["sqB"]
                pk = "sqA" if (lv - 1) % 2 else "sqB"
                lo = [(pt, pk, slice(0, 128)), (pt, pk, slice(256, 384))]
                up = [(pt, pk, slice(128, 256)), (pt, pk, slice(384, 512))]
            for hh in range(2):
                lt, lk, ls = lo[hh]
                ut, uk, us = up[hh]
                if lv < 6:
                    mm(bank[:, hh * 256:hh * 256 + 128], ut[:, us], lt[:, ls],
                       True, True, reads=(uk, lk), writes=(bkey,))
                mm(bank[:, hh * 256 + 128:hh * 256 + 256], lt[:, ls], ut[:, us],
                   True, True, reads=(lk, uk), writes=(bkey,))
            ncols = 512 if lv < 6 else None
            if lv < 6:
                vop("tensor_copy", sq_t[:], bank[:], reads=(bkey,), writes=(sq_k,))
            else:
                vop("tensor_copy", sq_t[:, 128:256], bank[:, 128:256],
                    reads=(bkey,), writes=(sq_k,))
                vop("tensor_copy", sq_t[:, 384:512], bank[:, 384:512],
                    reads=(bkey,), writes=(sq_k,))
            apply_level(sq_t, sq_k, slice(128, 256), slice(384, 512),
                        pe[lv % 2], PE_[lv % 2])

        if DBG_PSTEP < 8:
            return
        po = pm
        pok = PM
        for hh in range(2):
            mm(po[hh * 64:(hh + 1) * 64, 256:384], tl["U"][:, hh * 64:(hh + 1) * 64],
               tl["At"][:, hh * 128:(hh + 1) * 128], True, False,
               reads=("U", "At"), writes=(pok,), tp=(0, hh * 64))
            mm(po[hh * 64:(hh + 1) * 64, 256:384],
               tl["Sb"][b:b + 64, p * 128 + hh * 64: p * 128 + (hh + 1) * 64],
               tl["Qt"][b:b + 64, hh * 128:(hh + 1) * 128],
               False, True, reads=("Sb", "Qt"), writes=(pok,), tp=(b, hh * 64))
        vop("tensor_mul", tl["hpre"][:], po[:, 256:384],
            tl["zs"][:, p * TT + col0: p * TT + col0 + C],
            reads=(pok, "zs"), writes=("hpre",))
        if DBG_PSTEP < 9:
            return
        sq2 = sch.op("s", (lambda eng, o=tl["hsq"][:], i=tl["hpre"][:]:
                           eng.activation(o, i, AF.Square, bias=0.0, scale=1.0)),
                     reads=("hpre",), writes=("hsq",))
        mm(pm[0:2, 384:512], tl["blockhalf"][:], tl["hsq"][:], True, True,
           reads=("blockhalf", "hsq"), writes=(PM,))
        sact(tl["istd"][:], pm[0:2, 384:512], AF.Ln, tl["eps_rms"][:], 1.0 / 64,
             reads=(PM, "eps_rms"), writes=("istd",))
        sact(tl["istd"][:], tl["istd"][:], AF.Exp, 0.0, -0.5,
             reads=("istd",), writes=("istd",))
        vop("tensor_copy", tl["istdb"][:], tl["istd"][:], reads=("istd",), writes=("istdb",))
        mm(pm[:, 384:512], tl["halfind"][:], tl["istdb"][:], True, True,
           reads=("halfind", "istdb"), writes=(PM,))
        vop("scalar_tensor_tensor", tl["hb"][:, p * TT + col0: p * TT + col0 + C],
            tl["hpre"][:], tl["nwdup"][:], pm[:, 384:512], ALU.mult, ALU.mult,
            reads=("hpre", "nwdup", PM), writes=("hb",))

        if DBG_PSTEP < 10:
            return
        for hh in range(2):
            h = 2 * p + hh
            vop("tensor_scalar_mul", tl["U"][:, hh * 64:(hh + 1) * 64],
                tl["U"][:, hh * 64:(hh + 1) * 64], tl["ebcmbtm"][:, h:h + 1],
                reads=("U", "ebcmbtm"), writes=("U",))
        for hh in range(2):
            mm(pm[b:b + 64, hh * 64:(hh + 1) * 64],
               tl["Ktm"][:], tl["U"][:, hh * 64:(hh + 1) * 64],
               True, True, reads=("Ktm", "U"), writes=(PM,), tp=(0, b))
        for hh in range(2):
            h = 2 * p + hh
            vop("scalar_tensor_tensor",
                tl["S"][b:b + 64, p * 128 + hh * 64: p * 128 + (hh + 1) * 64],
                tl["S"][b:b + 64, p * 128 + hh * 64: p * 128 + (hh + 1) * 64],
                tl["ebc"][b:b + 64, h:h + 1], pm[b:b + 64, hh * 64:(hh + 1) * 64],
                ALU.mult, ALU.add, reads=("S", "ebc", PM), writes=("S",))
            vop("tensor_copy",
                tl["Sb"][b:b + 64, p * 128 + hh * 64: p * 128 + (hh + 1) * 64],
                tl["S"][b:b + 64, p * 128 + hh * 64: p * 128 + (hh + 1) * 64],
                reads=("S",), writes=("Sb",))

    def layer_norm(dest):
        for kc in range(8):
            vop("tensor_copy", tl["xb"][:, kc * TT:(kc + 1) * TT],
                tl["x32"][:, kc * TT:(kc + 1) * TT], reads=("x32",), writes=("xb",))
        for kc in range(8):
            sch.op("s", (lambda eng, o=tl["hb"][:, kc * TT:(kc + 1) * TT],
                         i=tl["xb"][:, kc * TT:(kc + 1) * TT]:
                         eng.activation(o, i, AF.Square, bias=0.0, scale=1.0)),
                   reads=("xb",), writes=("hb",))
        for kc in range(8):
            mm(pb[2][0:1, 0:TT], tl["onescol"][:], tl["xb"][:, kc * TT:(kc + 1) * TT],
               kc == 0, kc == 7, reads=("xb", "onescol"), writes=(PB[2],))
        for kc in range(8):
            mm(pb[2][0:1, 256:256 + TT], tl["onescol"][:], tl["hb"][:, kc * TT:(kc + 1) * TT],
               kc == 0, kc == 7, reads=("hb", "onescol"), writes=(PB[2],))
        vop("tensor_scalar_mul", tl["lnrow"][:, 0:TT], pb[2][0:1, 0:TT], 1.0 / D,
            reads=(PB[2],), writes=("lnrow",))
        vop("tensor_scalar_mul", tl["lnrow"][:, TT:2 * TT], pb[2][0:1, 256:256 + TT], 1.0 / D,
            reads=(PB[2],), writes=("lnrow",))
        vop("tensor_copy", tl["lnb"][:, 0:TT], tl["lnrow"][:, 0:TT],
            reads=("lnrow",), writes=("lnb",))
        vop("tensor_mul", tl["lnrow"][:, 0:TT], tl["lnrow"][:, 0:TT],
            tl["lnrow"][:, 0:TT], reads=("lnrow",), writes=("lnrow",))
        vop("tensor_sub", tl["lnrow"][:, TT:2 * TT], tl["lnrow"][:, TT:2 * TT],
            tl["lnrow"][:, 0:TT], reads=("lnrow",), writes=("lnrow",))
        sact(tl["lnrow"][:, TT:2 * TT], tl["lnrow"][:, TT:2 * TT], AF.Ln,
             tl["eps_ln"][:], 1.0, reads=("lnrow", "eps_ln"), writes=("lnrow",))
        sact(tl["lnrow"][:, TT:2 * TT], tl["lnrow"][:, TT:2 * TT], AF.Exp,
             0.0, -0.5, reads=("lnrow",), writes=("lnrow",))
        vop("tensor_copy", tl["lnb"][:, TT:2 * TT], tl["lnrow"][:, TT:2 * TT],
            reads=("lnrow",), writes=("lnb",))

        mm(pb[3][:, 0:TT], tl["onesrow"][:], tl["lnb"][:, 0:TT], True, True,
           reads=("lnb", "onesrow"), writes=(PB[3],))
        mm(pb[3][:, 256:256 + TT], tl["onesrow"][:], tl["lnb"][:, TT:2 * TT], True, True,
           reads=("lnb", "onesrow"), writes=(PB[3],))
        for kc in range(8):
            vop("tensor_sub", tl["ftmp"][:], tl["x32"][:, kc * TT:(kc + 1) * TT],
                pb[3][:, 0:TT], reads=("x32", PB[3]), writes=("ftmp",))
            vop("tensor_mul", tl["x32"][:, kc * TT:(kc + 1) * TT], tl["ftmp"][:],
                pb[3][:, 256:256 + TT], reads=("ftmp", PB[3]), writes=("x32",))
        if dest is not None:
            for kc in range(8):
                scopy(tl[dest][:, kc * TT:(kc + 1) * TT], tl["x32"][:, kc * TT:(kc + 1) * TT],
                      reads=("x32",), writes=(dest,))

    def emit_tile(t):
        x_sl = dr["xT"][:, t * TT:(t + 1) * TT].rearrange("(c p) j -> p c j", p=128)
        dma("x", tl["x32"][:].rearrange("p (c j) -> p c j", c=8), x_sl,
            reads=("x32",), writes=("x32",))
        m_sl = dr["maskT"][:, t * TT:(t + 1) * TT].rearrange("(c p) j -> p c j", p=128)
        dma("m", tl["msk"][:].rearrange("p (c j) -> p c j", c=8), m_sl,
            reads=("msk",), writes=("msk",), queue="g")
        for kc in range(8):
            vop("tensor_copy", tl["xb"][:, kc * TT:(kc + 1) * TT],
                tl["x32"][:, kc * TT:(kc + 1) * TT], reads=("x32",), writes=("xb",))

        if DBG_STAGE == "prol":
            return

        def dense(oc, nkc, wt, wkey, stride, bank):
            for kc in range(nkc):
                mm(pb[bank][:, 0:TT], tl[wt][:, kc * stride + oc * 128: kc * stride + (oc + 1) * 128],
                   tl["xb" if wt in ("wqkv", "wz", "wba") else
                      ("hb" if wt == "wout" else ("xq" if wt == "w1" else "h1"))]
                   [:, kc * TT:(kc + 1) * TT],
                   kc == 0, kc == nkc - 1,
                   reads=(wkey, "xb" if wt in ("wqkv", "wz", "wba") else
                          ("hb" if wt == "wout" else ("xq" if wt == "w1" else "h1"))),
                   writes=(PB[bank],))

        for oc in range(16):
            bank = oc % 2
            dense(oc, 8, "wqkv", "wqkv", CONV, bank)
            scopy(mxc(oc, 3, 3 + TT), pb[bank][:, 0:TT], reads=(PB[bank],), writes=("mixed",))
        for oc in range(8):
            bank = oc % 2
            dense(oc, 8, "wz", "wz", VAL, bank)
            sact(tl["zs"][:, oc * TT:(oc + 1) * TT], pb[bank][:, 0:TT], AF.Silu, 0.0, 1.0,
                 reads=(PB[bank],), writes=("zs",))
        for kc in range(8):
            mm(pb[2][0:48, 0:TT], tl["wba"][:, kc * 48:(kc + 1) * 48],
               tl["xb"][:, kc * TT:(kc + 1) * TT], kc == 0, kc == 7,
               reads=("xb", "wba"), writes=(PB[2],))
        vop("tensor_copy", tl["bat"][:, 0:TT], pb[2][0:48, 0:TT],
            reads=(PB[2],), writes=("bat",))

        if DBG_STAGE == "proj":
            return
        for c in range(16):
            cw = lambda j: tl["convw"][:, c * 4 + j: c * 4 + j + 1]
            vop("tensor_scalar_mul", tl["cvt"][:], mxc(c, 0, TT), cw(0),
                reads=("mixed", "convw"), writes=("cvt",))
            for j in (1, 2, 3):
                vop("scalar_tensor_tensor", tl["cvt"][:], mxc(c, j, j + TT), cw(j),
                    tl["cvt"][:], ALU.mult, ALU.add,
                    reads=("mixed", "convw", "cvt"), writes=("cvt",))
            vop("tensor_copy", mxc(c, 0, 3), mxc(c, TT, TT + 3),
                reads=("mixed",), writes=("mixed",))
            sact(mxc(c, 3, 3 + TT), tl["cvt"][:], AF.Silu, 0.0, 1.0,
                 reads=("cvt",), writes=("mixed",))

        if DBG_STAGE == "conv":
            return
        sact(tl["gsc"][:, 0:TT], tl["bat"][0:16, 0:TT], AF.Exp, tl["dtb16"][:], 1.0,
             reads=("bat", "dtb16"), writes=("gsc",))
        vop("tensor_scalar_add", tl["gsc"][:, 0:TT], tl["gsc"][:, 0:TT], 1.0,
            reads=("gsc",), writes=("gsc",))
        sact(tl["ga"][:, 0:TT], tl["gsc"][:, 0:TT], AF.Ln, 0.0, 1.0,
             reads=("gsc",), writes=("ga",))
        vop("tensor_scalar_mul", tl["ga"][:, 0:TT], tl["ga"][:, 0:TT], tl["negA"][:],
            reads=("ga", "negA"), writes=("ga",))

        if DBG_STAGE == "ga":
            return
        for cc in range(NCH if DBG_CHUNKS is None else DBG_CHUNKS):
            emit_chunk(t, cc)
        if DBG_SCAN_ONLY or DBG_STAGE == "scan":
            return

        for oc in range(8):
            bank = oc % 2
            dense(oc, 8, "wout", "wout", D, bank)
            vop("tensor_add", tl["x32"][:, oc * TT:(oc + 1) * TT], pb[bank][:, 0:TT],
                tl["x32"][:, oc * TT:(oc + 1) * TT], reads=(PB[bank], "x32"), writes=("x32",))
        layer_norm("xq")
        if DBG_STOP_LN1:
            return

        for oc1 in range(32):
            bank = oc1 % 2
            dense(oc1, 8, "w1", "w1", H1, bank)
            sact(tl["f16t"][:], pb[bank][:, 0:TT], AF.Relu, tl["b1q"][:, oc1:oc1 + 1], 1.0,
                 reads=(PB[bank], "b1q"), writes=("f16t",))
            vop("tensor_copy", tl["h1"][:, oc1 * TT:(oc1 + 1) * TT], tl["f16t"][:],
                reads=("f16t",), writes=("h1",))
        for oc2 in range(8):
            bank = oc2 % 2
            dense(oc2, 32, "w2", "w2", D, bank)
            sact(tl["f16t"][:], pb[bank][:, 0:TT], AF.Identity, tl["b2q"][:, oc2:oc2 + 1], 1.0,
                 reads=(PB[bank], "b2q"), writes=("f16t",))
            vop("tensor_mul", tl["ftmp"][:], tl["f16t"][:], tl["msk"][:, oc2 * TT:(oc2 + 1) * TT],
                reads=("f16t", "msk"), writes=("ftmp",))
            vop("tensor_add", tl["x32"][:, oc2 * TT:(oc2 + 1) * TT], tl["ftmp"][:],
                tl["x32"][:, oc2 * TT:(oc2 + 1) * TT], reads=("ftmp", "x32"), writes=("x32",))
        layer_norm(None)
        o_sl = dr["outT"][:, t * TT:(t + 1) * TT].rearrange("(c p) j -> p c j", p=128)
        dma("o", o_sl, tl["x32"][:].rearrange("p (c j) -> p c j", c=8),
            reads=("x32",), writes=())

    for t in range(N_TILES_EMIT):
        emit_tile(t)

    sch.finalize()
    with ctx:
        sch.emit(nc, ctx)
    return nc


_NC_CACHE = {}


def _build_cached():
    key = (N_TILES_EMIT, DEBUG_TAPS)
    if key not in _NC_CACHE:
        _NC_CACHE[key] = build_program()
    return _NC_CACHE[key]


def kernel(L0_attn_hidden_states, L0_attn_W_qkv, L0_attn_conv_w, L0_attn_W_z,
           L0_attn_W_b, L0_attn_W_a, L0_attn_dt_bias, L0_attn_A_log,
           L0_attn_norm_weight, L0_attn_W_out, L0_ffn_w1, L0_ffn_b1,
           L0_ffn_w2, L0_ffn_b2):
    import jax

    x = np.asarray(L0_attn_hidden_states, dtype=np.float32)
    wqkv = np.asarray(L0_attn_W_qkv, dtype=np.float32)
    convw = np.asarray(L0_attn_conv_w, dtype=np.float32)
    wz = np.asarray(L0_attn_W_z, dtype=np.float32)
    wb = np.asarray(L0_attn_W_b, dtype=np.float32)
    wa = np.asarray(L0_attn_W_a, dtype=np.float32)
    dtb = np.asarray(L0_attn_dt_bias, dtype=np.float32)
    alog = np.asarray(L0_attn_A_log, dtype=np.float32)
    nw = np.asarray(L0_attn_norm_weight, dtype=np.float32)
    wout = np.asarray(L0_attn_W_out, dtype=np.float32)
    w1 = np.asarray(L0_ffn_w1, dtype=np.float32)
    b1 = np.asarray(L0_ffn_b1, dtype=np.float32)
    w2 = np.asarray(L0_ffn_w2, dtype=np.float32)
    b2 = np.asarray(L0_ffn_b2, dtype=np.float32)

    with jax.default_device(jax.devices("cpu")[0]):
        keep = np.asarray(jax.random.bernoulli(jax.random.key(42, impl="threefry2x32"), 0.8, x.shape))
    maskv = np.where(keep, np.float32(1.25), np.float32(0.0))

    wba = np.zeros((48, D), np.float32)
    wba[0:16] = wa
    wba[32:48] = wb
    shared = {
        "wqkvT": np.ascontiguousarray(wqkv.T),
        "wzT": np.ascontiguousarray(wz.T),
        "wbaT": np.ascontiguousarray(wba.T),
        "woutT": np.ascontiguousarray(wout.T),
        "w1T": np.ascontiguousarray(w1.T),
        "w2T": np.ascontiguousarray(w2.T),
        "convw": np.ascontiguousarray(
            convw[:, 0, :].reshape(16, 128, 4).transpose(1, 0, 2).reshape(128, 64)),
        "b1r": np.ascontiguousarray(b1.reshape(32, 128).T),
        "b2r": np.ascontiguousarray(b2.reshape(8, 128).T),
        "alog16": np.ascontiguousarray(alog.reshape(16, 1)),
        "dtb16": np.ascontiguousarray(dtb.reshape(16, 1)),
        "nwdup": np.ascontiguousarray(np.tile(nw, 2).reshape(128, 1)),
    }
    in_maps = []
    for c in range(NCORE):
        m = dict(shared)
        m["xT"] = np.ascontiguousarray(x[c].T)
        m["maskT"] = np.ascontiguousarray(maskv[c].T.astype(np.float32))
        in_maps.append(m)

    nc = _build_cached()
    res = run_bass_kernel_spmd(nc, in_maps, core_ids=list(range(NCORE)))
    out = np.empty((NCORE, T, D), dtype=np.float32)
    for c in range(NCORE):
        out[c] = res.results[c]["outT"].T
    return out


# version 02
# speedup vs baseline: 2.0423x; 2.0423x over previous
import math
import numpy as np
import ml_dtypes
from contextlib import ExitStack

import concourse.bass as bass
import concourse.mybir as mybir
from concourse.bass_utils import run_bass_kernel_spmd

F32 = mybir.dt.float32
BF16 = mybir.dt.bfloat16
FP16 = mybir.dt.float16
FP8 = mybir.dt.float8e4
U16 = mybir.dt.uint16
AF = mybir.ActivationFunctionType
ALU = mybir.AluOpType

NCORE = 8
T = 4096
D = 1024
TT = 256
NT = T // TT
C = 128
NCH = TT // C
CONV = 2048
VAL = 1024
NPAIR = 8
H1 = 4096
PEN = 1.0e5
EPS_LN = 1e-5
EPS_RMS = 1e-6

N_TILES_EMIT = NT
DBG_SKIP = set()
DBG_PSTEP = 99
DBG_SCAN_ONLY = False
DBG_STOP_LN1 = False
DBG_STAGE = None
DBG_CHUNKS = None
DBG_PAIRS = None
DEBUG_TAPS = ()


class Op:
    __slots__ = ("eng", "fn", "need_inc", "sem", "amt", "waits", "idx_hint",
                 "cum", "note")

    def __init__(self, eng, fn, sem, amt, note=""):
        self.eng = eng
        self.fn = fn
        self.need_inc = False
        self.sem = sem
        self.amt = amt
        self.waits = {}
        self.note = note


class Sched:
    ENGINES = ("q", "g", "v", "s", "t")

    def __init__(self):
        self.ops = []
        self.res = {}
        self.dma_seq = {}
        self.eng_last = {}

    def _edge(self, prod, cons):
        if prod is None or prod is cons:
            return
        if prod.eng == cons.eng and prod.eng == "t":
            return
        prod.need_inc = True
        cur = cons.waits.get(prod.sem)
        if cur is None or cur.idx_hint < prod.idx_hint:
            cons.waits[prod.sem] = prod

    def _track(self, o, reads, writes):
        o.idx_hint = len(self.ops)
        self.ops.append(o)
        for k in reads:
            st = self.res.setdefault(k, {"w": None, "r": []})
            self._edge(st["w"], o)
            if k.startswith("PB") or k.startswith("PQ"):
                for r in st["r"]:
                    if r.eng != o.eng:
                        self._edge(r, o)
            st["r"].append(o)
        for k in writes:
            st = self.res.setdefault(k, {"w": None, "r": []})
            self._edge(st["w"], o)
            for r in st["r"]:
                self._edge(r, o)
            st["w"] = o
            st["r"] = []
        self.eng_last[o.eng] = o
        return o

    def op(self, eng, fn, reads=(), writes=(), note=""):
        return self._track(Op(eng, fn, "sem_" + eng, 1, note), reads, writes)

    def dma(self, stream, fn, reads=(), writes=(), note="", queue="q"):
        prev = self.dma_seq.get(stream)
        o = Op(queue, fn, f"dma_{stream}", 16, note)
        o.need_inc = True
        self._track(o, reads, writes)
        if prev is not None:
            o.waits[o.sem] = prev
        self.dma_seq[stream] = o
        return o

    def finalize(self):
        counts = {}
        for o in self.ops:
            if o.need_inc:
                counts[o.sem] = counts.get(o.sem, 0) + o.amt
                o.cum = counts[o.sem]
        self.sem_totals = counts

    def emit(self, nc, ctx):
        sems = {}
        names = sorted({o.sem for o in self.ops if o.need_inc}
                       | {o2.sem for o in self.ops for o2 in o.waits.values()})
        for n in names:
            sems[n] = ctx.enter_context(nc.semaphore(n))
        per_eng = {e: [] for e in self.ENGINES}
        for o in self.ops:
            per_eng[o.eng].append(o)
        final_waits = [(sems[n], tot) for n, tot in sorted(self.sem_totals.items())
                       if n.startswith("dma_o_")]

        def mk_body(eng_name, add_drain):
            ops = per_eng[eng_name]

            def body(eng):
                hw = {}
                for o in ops:
                    for sem_name in sorted(o.waits):
                        v = o.waits[sem_name].cum
                        if hw.get(sem_name, 0) < v:
                            eng.wait_ge(sems[sem_name], v)
                            hw[sem_name] = v
                    ins = o.fn(eng)
                    if o.need_inc:
                        ins.then_inc(sems[o.sem], o.amt)
                if add_drain:
                    for s, tot in final_waits:
                        eng.wait_ge(s, tot)
            return body

        with nc.Block() as block:
            block.sync(mk_body("q", False))
            if per_eng["g"]:
                block.gpsimd(mk_body("g", False))
            block.vector(mk_body("v", False))
            block.scalar(mk_body("s", False))
            block.tensor(mk_body("t", True))


def build_program():
    nc = bass.Bass("TRN2", target_bir_lowering=False)
    sch = Sched()

    dr = {}
    dr["xT"] = nc.dram_tensor("xT", [D, T], F32, kind="ExternalInput").ap()
    dr["maskT"] = nc.dram_tensor("maskT", [D, T], F32, kind="ExternalInput").ap()
    dr["wqkvT"] = nc.dram_tensor("wqkvT", [D, CONV], F32, kind="ExternalInput").ap()
    dr["wzT"] = nc.dram_tensor("wzT", [D, VAL], F32, kind="ExternalInput").ap()
    dr["wbaT"] = nc.dram_tensor("wbaT", [D, 48], F32, kind="ExternalInput").ap()
    dr["woutT"] = nc.dram_tensor("woutT", [VAL, D], F32, kind="ExternalInput").ap()
    dr["w1T"] = nc.dram_tensor("w1T", [D, H1], F32, kind="ExternalInput").ap()
    dr["w2T"] = nc.dram_tensor("w2T", [H1, D], F32, kind="ExternalInput").ap()
    dr["convw"] = nc.dram_tensor("convw", [128, 64], F32, kind="ExternalInput").ap()
    dr["b1r"] = nc.dram_tensor("b1r", [128, 32], F32, kind="ExternalInput").ap()
    dr["b2r"] = nc.dram_tensor("b2r", [128, 8], F32, kind="ExternalInput").ap()
    dr["alog16"] = nc.dram_tensor("alog16", [16, 1], F32, kind="ExternalInput").ap()
    dr["dtb16"] = nc.dram_tensor("dtb16", [16, 1], F32, kind="ExternalInput").ap()
    dr["nwdup"] = nc.dram_tensor("nwdup", [128, 1], F32, kind="ExternalInput").ap()
    dr["outT"] = nc.dram_tensor("outT", [D, T], F32, kind="ExternalOutput").ap()
    taps = {}
    for name in DEBUG_TAPS:
        taps[name] = nc.dram_tensor("tap_" + name, [128, 8192], F32,
                                    kind="ExternalOutput").ap()

    def bf(x):
        return np.asarray(x, dtype=ml_dtypes.bfloat16).view(np.uint16)

    ji, ii = np.meshgrid(np.arange(128), np.arange(128), indexing="ij")
    cn = {}
    cn["id_b"] = nc.inline_tensor(bf(np.eye(128)), name="c0")
    cn["id_f"] = nc.inline_tensor(np.eye(128, dtype=np.float32), name="c1")
    cn["utones"] = nc.inline_tensor(bf((ji <= ii).astype(np.float32)), name="c2")
    cn["penL"] = nc.inline_tensor(bf(np.where(ii > ji, PEN, 0.0)), name="c3")
    cn["offdiag"] = nc.inline_tensor(bf(1.0 - np.eye(128)), name="c4")
    cn["onesrow"] = nc.inline_tensor(bf(np.ones((1, 128))), name="c5")
    cn["onescol"] = nc.inline_tensor(bf(np.ones((128, 1))), name="c6")
    hind = np.zeros((2, 128), np.float32)
    hind[0, 0:64] = 1.0
    hind[1, 64:128] = 1.0
    cn["halfind"] = nc.inline_tensor(bf(hind), name="c7")
    bh = np.zeros((128, 2), np.float32)
    bh[0:64, 0] = 1.0
    bh[64:128, 1] = 1.0
    cn["blockhalf"] = nc.inline_tensor(bf(bh), name="c8")
    hs = np.zeros((16, 16 * 128), np.float32)
    for h in range(16):
        hs[h, h * 128:(h + 1) * 128] = 1.0
    cn["hsel"] = nc.inline_tensor(bf(hs), name="c9")

    ctx = ExitStack()
    sb = lambda n, shp, dt: ctx.enter_context(nc.sbuf_tensor(n, shp, dt))
    ps = lambda n, shp, dt=F32: ctx.enter_context(nc.psum_tensor(n, shp, dt))

    tl = {}
    for k, shp, dt in [
        ("id_b", [128, 128], BF16), ("id_f", [128, 128], F32),
        ("utones", [128, 128], BF16), ("penL", [128, 128], BF16),
        ("offdiag", [128, 128], BF16), ("onesrow", [1, 128], BF16),
        ("onescol", [128, 1], BF16), ("halfind", [2, 128], BF16),
        ("blockhalf", [128, 2], BF16), ("hsel", [16, 16 * 128], BF16),
        ("convw", [128, 64], F32), ("b1r", [128, 32], F32), ("b2r", [128, 8], F32),
        ("alog16", [16, 1], F32), ("dtb16", [16, 1], F32), ("negA", [16, 1], F32),
        ("nwdup", [128, 1], F32), ("eps_ln", [1, 1], F32), ("eps_rms", [2, 1], F32),
        ("ln8", [16, 1], F32),
        ("wqkv", [128, 8 * CONV], BF16), ("wz", [128, 8 * VAL], BF16),
        ("wba", [128, 8 * 48], BF16), ("wout", [128, 8 * D], BF16),
        ("w1", [128, 8 * H1], FP8), ("w2", [128, 32 * D], FP8),
        ("b1q", [128, 32], F32), ("b2q", [128, 8], F32), ("bq8", [128, 32], FP8),
        ("x32", [128, 8 * TT], F32), ("xb", [128, 8 * TT], BF16),
        ("mixed", [128, 16 * (TT + 3)], BF16), ("zs", [128, 8 * TT], BF16),
        ("bat", [48, TT], F32), ("ga", [16, TT], F32), ("gsc", [16, TT], F32),
        ("cvt", [128, TT], F32), ("hb", [128, 8 * TT], BF16),
        ("h1", [128, 32 * TT], FP8),
        ("f16t", [128, TT], FP16), ("ftmp", [128, TT], F32),
        ("msk", [128, 4 * TT], BF16),
        ("lnrow", [1, 2 * TT], F32), ("lnb", [1, 2 * TT], BF16),
        ("S", [128, NPAIR * 128], F32), ("Sb", [128, NPAIR * 128], BF16),
        ("btm", [128, 16], F32), ("gtmh", [128, 16], BF16), ("gtml", [128, 16], BF16),
        ("gtmp", [128, 16], F32), ("brows", [16, 128], F32), ("bsp", [16, 3 * 128], BF16),
        ("bsc", [16, 2 * 128], F32),
        ("betatm", [128, 16], F32), ("negebtm", [128, 16], F32),
        ("ebtm", [128, 16], F32), ("ebcmbfm", [16, 128], F32), ("ebcmbtm", [128, 16], F32),
        ("ebqrowb", [16, 128], BF16), ("bcrow", [1, 16], F32), ("ebcrow", [1, 16], F32),
        ("ebcrowb", [1, 2 * 16], BF16), ("ebc", [128, 16], F32),
        ("E", [128, 512], BF16), ("KKz", [128, 512], BF16), ("KQ", [128, 256], BF16),
        ("A", [128, 512], BF16), ("At", [128, 512], BF16),
        ("N0", [128, 512], BF16), ("Nt", [128, 512], BF16),
        ("sqA", [128, 1024], BF16), ("sqB", [128, 1024], BF16),
        ("Vtm", [128, 256], BF16), ("Ktm", [128, 128], BF16), ("Qt", [128, 256], BF16),
        ("rpre", [128, 128], F32), ("U", [128, 256], BF16),
        ("hpre", [128, 256], F32), ("hsq", [128, 256], BF16),
        ("istd", [2, 256], F32), ("istdb", [2, 256], BF16),
    ]:
        tl[k] = sb("t_" + k, shp, dt)
    tl["xq"] = nc.alloc_sbuf_tensor_at(
        "t_xq", [128, 8 * TT], FP8, offset=nc.lookup_mloc(tl["hb"]).addr)
    tl["stg"] = nc.alloc_sbuf_tensor_at(
        "t_stg", [128, 512], F32, offset=nc.lookup_mloc(tl["h1"]).addr)

    pb = [ps(f"pb{i}", [128, 512], F32) for i in range(6)]
    pq = [ps(f"pq{i}", [128, 1024], BF16) for i in range(2)]
    PB = [f"PB{i}" for i in range(6)]
    PQ = [f"PQ{i}" for i in range(2)]

    def mm(out_ap, lhsT_ap, rhs_ap, start, stop, reads, writes, tp=None):
        def fn(eng, o=out_ap, l=lhsT_ap, r=rhs_ap, st=start, sp=stop, tpos=tp):
            return eng.matmul(o, l, r, start=st, stop=sp, tile_position=tpos)
        return sch.op("t", fn, reads=reads, writes=writes)

    def tpose(out_ap, in_ap, ident_ap, reads, writes):
        def fn(eng, o=out_ap, i=in_ap, d=ident_ap):
            return eng.transpose(o, i, d)
        return sch.op("t", fn, reads=reads, writes=writes)

    def vop(name, *args, reads=(), writes=()):
        def fn(eng, n=name, a=args):
            return getattr(eng, n)(*a)
        return sch.op("v", fn, reads=reads, writes=writes)

    def sact(out_ap, in_ap, func, bias, scale, reads, writes):
        def fn(eng, o=out_ap, i=in_ap, f=func, b=bias, s=scale):
            return eng.activation(o, i, f, bias=b, scale=s)
        return sch.op("s", fn, reads=reads, writes=writes)

    def scopy(out_ap, in_ap, reads, writes):
        def fn(eng, o=out_ap, i=in_ap):
            return eng.copy(o, i)
        return sch.op("s", fn, reads=reads, writes=writes)

    def dma(stream, out_ap, in_ap, reads, writes, queue="q"):
        def fn(eng, o=out_ap, i=in_ap):
            return eng.dma_start(out=o, in_=i)
        return sch.dma(stream, fn, reads=reads, writes=writes, queue=queue)

    def mxc(c, a, b2_):
        return tl["mixed"][:, c * (TT + 3) + a: c * (TT + 3) + b2_]

    for k in ("id_b", "utones", "penL", "offdiag", "onesrow", "onescol",
              "halfind", "blockhalf", "hsel"):
        dma("c", tl[k][:], cn[k].ap().bitcast(BF16), reads=(), writes=(k,))
    dma("c", tl["id_f"][:], cn["id_f"].ap(), reads=(), writes=("id_f",))
    for k in ("convw", "b1r", "b2r", "alog16", "dtb16", "nwdup"):
        dma("c", tl[k][:], dr[k], reads=(), writes=(k,))

    vop("memset", tl["eps_ln"][:], EPS_LN, writes=("eps_ln",))
    vop("memset", tl["eps_rms"][:], EPS_RMS, writes=("eps_rms",))
    vop("memset", tl["ln8"][:], math.log(0.125), writes=("ln8",))
    vop("memset", tl["S"][:], 0.0, writes=("S0", "S1"))
    vop("memset", tl["Sb"][:], 0.0, writes=("Sb0", "Sb1"))
    for c in range(16):
        vop("memset", mxc(c, 0, 3), 0.0, writes=("mixed",))
    sact(tl["negA"][:], tl["alog16"][:], AF.Exp, 0.0, 1.0,
         reads=("alog16",), writes=("negA",))
    vop("tensor_scalar_mul", tl["negA"][:], tl["negA"][:], -1.0,
        reads=("negA",), writes=("negA",))

    def stage_cast_big(dst_tile, dst_off, src_ap2d, wkey, cols):
        for half in range(0, cols, 512):
            w = min(512, cols - half)
            dma("w", tl["stg"][:, 0:w], src_ap2d[:, half:half + w],
                reads=("h1",), writes=("h1",))
            vop("tensor_copy", dst_tile[:, dst_off + half: dst_off + half + w],
                tl["stg"][:, 0:w], reads=("h1",), writes=(wkey,))

    for kc in range(8):
        stage_cast_big(tl["wqkv"], kc * CONV,
                       dr["wqkvT"][kc * 128:(kc + 1) * 128, :], "wqkv", CONV)
        stage_cast_big(tl["wz"], kc * VAL,
                       dr["wzT"][kc * 128:(kc + 1) * 128, :], "wz", VAL)
        stage_cast_big(tl["wba"], kc * 48,
                       dr["wbaT"][kc * 128:(kc + 1) * 128, :], "wba", 48)
        stage_cast_big(tl["wout"], kc * D,
                       dr["woutT"][kc * 128:(kc + 1) * 128, :], "wout", D)
        stage_cast_big(tl["w1"], kc * H1,
                       dr["w1T"][kc * 128:(kc + 1) * 128, :], "w1", H1)
        for sub in range(4):
            kc2 = kc * 4 + sub
            stage_cast_big(tl["w2"], kc2 * D,
                           dr["w2T"][kc2 * 128:(kc2 + 1) * 128, :], "w2", D)
    vop("tensor_copy", tl["bq8"][:, 0:32], tl["b1r"][:], reads=("b1r",), writes=("bq8",))
    vop("tensor_copy", tl["b1q"][:], tl["bq8"][:, 0:32], reads=("bq8",), writes=("b1q",))
    vop("tensor_copy", tl["bq8"][:, 0:8], tl["b2r"][:], reads=("b2r",), writes=("bq8",))
    vop("tensor_copy", tl["b2q"][:], tl["bq8"][:, 0:8], reads=("bq8",), writes=("b2q",))

    def emit_chunk(t, cc):
        col0 = cc * C
        idf16 = tl["id_f"][0:16, 0:16]
        tpose(pb[4][0:128, 0:16], tl["ga"][:, col0:col0 + C], idf16,
              reads=("ga", "id_f"), writes=(PB[4],))
        vop("tensor_copy", tl["gtmh"][:], pb[4][0:128, 0:16], reads=(PB[4],), writes=("gtmh",))
        vop("tensor_sub", tl["gtmp"][:], pb[4][0:128, 0:16], tl["gtmh"][:],
            reads=(PB[4], "gtmh"), writes=("gtmp",))
        vop("tensor_copy", tl["gtml"][:], tl["gtmp"][:], reads=("gtmp",), writes=("gtml",))
        mm(pb[5][0:128, 0:16], tl["utones"][:], tl["gtmh"][:], True, False,
           reads=("gtmh", "utones"), writes=(PB[5],))
        mm(pb[5][0:128, 0:16], tl["utones"][:], tl["gtml"][:], False, True,
           reads=("gtml", "utones"), writes=(PB[5],))
        vop("tensor_copy", tl["btm"][:], pb[5][0:128, 0:16], reads=(PB[5],), writes=("btm",))
        tpose(pb[4][0:16, 16:144], tl["btm"][:], tl["id_f"][:],
              reads=("btm", "id_f"), writes=(PB[4],))
        vop("tensor_copy", tl["brows"][:], pb[4][0:16, 16:144], reads=(PB[4],), writes=("brows",))
        vop("tensor_copy", tl["bsp"][:, 0:128], tl["brows"][:], reads=("brows",), writes=("bsp",))
        vop("tensor_sub", tl["bsc"][:, 0:128], tl["brows"][:], tl["bsp"][:, 0:128],
            reads=("brows", "bsp"), writes=("bsc",))
        vop("tensor_copy", tl["bsp"][:, 128:256], tl["bsc"][:, 0:128],
            reads=("bsc",), writes=("bsp",))
        vop("tensor_sub", tl["bsc"][:, 128:256], tl["bsc"][:, 0:128], tl["bsp"][:, 128:256],
            reads=("bsc", "bsp"), writes=("bsc",))
        vop("tensor_copy", tl["bsp"][:, 256:384], tl["bsc"][:, 128:256],
            reads=("bsc",), writes=("bsp",))
        tpose(pb[5][0:128, 16:32], tl["bat"][32:48, col0:col0 + C],
              tl["id_f"][32:48, 32:48], reads=("bat", "id_f"), writes=(PB[5],))
        sact(tl["betatm"][:], pb[5][0:128, 16:32], AF.Sigmoid, 0.0, 1.0,
             reads=(PB[5],), writes=("betatm",))
        sact(tl["ebtm"][:], tl["btm"][:], AF.Exp, 0.0, 1.0, reads=("btm",), writes=("ebtm",))
        vop("tensor_scalar_mul", tl["negebtm"][:], tl["ebtm"][:], -1.0,
            reads=("ebtm",), writes=("negebtm",))
        sact(tl["ebcmbfm"][:], tl["brows"][:], AF.Exp, tl["brows"][:, 127:128], -1.0,
             reads=("brows",), writes=("ebcmbfm",))
        tpose(pb[5][0:128, 32:48], tl["ebcmbfm"][:], idf16,
              reads=("ebcmbfm", "id_f"), writes=(PB[5],))
        vop("tensor_copy", tl["ebcmbtm"][:], pb[5][0:128, 32:48],
            reads=(PB[5],), writes=("ebcmbtm",))
        sact(tl["ebqrowb"][:], tl["brows"][:], AF.Exp, tl["ln8"][:], 1.0,
             reads=("brows", "ln8"), writes=("ebqrowb",))
        tpose(pb[5][0:1, 48:64], tl["brows"][:, 127:128], idf16,
              reads=("brows", "id_f"), writes=(PB[5],))
        sact(tl["ebcrow"][:], pb[5][0:1, 48:64], AF.Exp, 0.0, 1.0,
             reads=(PB[5],), writes=("ebcrow",))
        vop("tensor_copy", tl["ebcrowb"][:, 0:16], tl["ebcrow"][:],
            reads=("ebcrow",), writes=("ebcrowb",))
        vop("tensor_sub", tl["bcrow"][:], tl["ebcrow"][:], tl["ebcrowb"][:, 0:16],
            reads=("ebcrow", "ebcrowb"), writes=("bcrow",))
        vop("tensor_copy", tl["ebcrowb"][:, 16:32], tl["bcrow"][:],
            reads=("bcrow",), writes=("ebcrowb",))
        mm(pb[5][0:128, 64:80], tl["onesrow"][:], tl["ebcrowb"][:, 0:16], True, False,
           reads=("ebcrowb", "onesrow"), writes=(PB[5],))
        mm(pb[5][0:128, 64:80], tl["onesrow"][:], tl["ebcrowb"][:, 16:32], False, True,
           reads=("ebcrowb", "onesrow"), writes=(PB[5],))
        vop("tensor_copy", tl["ebc"][:], pb[5][0:128, 64:80], reads=(PB[5],), writes=("ebc",))

        for blk in range(NPAIR // 2 if DBG_PAIRS is None else DBG_PAIRS):
            emit_pair_phases(t, cc, blk)

    def emit_pair_phases(t, cc, blk):
        col0 = cc * C
        P2 = [2 * blk, 2 * blk + 1]

        def ctxof(p):
            q = p % 2
            return dict(
                p=p, q=q, b=64 * q,
                qrows=mxc(p // 2, 3 + col0, 3 + col0 + C)[64 * q:64 * q + 64, :],
                krows=mxc(4 + p // 2, 3 + col0, 3 + col0 + C)[64 * q:64 * q + 64, :],
                vpair=mxc(8 + p, 3 + col0, 3 + col0 + C),
                pm=pb[4 + q], PM=PB[4 + q], pq_=pq[q], PQ_=PQ[q],
            )

        CX = [ctxof(p) for p in P2]

        def sl(name, q, w):
            return tl[name][:, q * w:(q + 1) * w], name + str(q)

        def evac(q, out_ap, in_ap, reads, writes):
            if q == 0:
                vop("tensor_copy", out_ap, in_ap, reads=reads, writes=writes)
            else:
                scopy(out_ap, in_ap, reads=reads, writes=writes)

        for c in CX:
            mm(c["pm"][:, 0:128], c["krows"], c["krows"], True, True,
               reads=("mixed",), writes=(c["PM"],))
            mm(c["pm"][:, 128:256], c["qrows"], c["krows"], True, True,
               reads=("mixed",), writes=(c["PM"],))
        for c in CX:
            q = c["q"]
            kkz, KKZ = sl("KKz", q, 256)
            kq, KQk = sl("KQ", q, 128)
            for hh in range(2):
                h = 2 * c["p"] + hh
                vop("tensor_scalar", kkz[:, hh * 128:(hh + 1) * 128], c["pm"][:, 0:128],
                    tl["betatm"][:, h:h + 1], -1.0, ALU.mult, ALU.mult,
                    reads=(c["PM"], "betatm"), writes=(KKZ,))
            sact(kq, c["pm"][:, 128:256], AF.Identity, 0.0, 0.125,
                 reads=(c["PM"],), writes=(KQk,))
            for hh in range(2):
                vop("tensor_mul", kkz[:, hh * 128:(hh + 1) * 128],
                    kkz[:, hh * 128:(hh + 1) * 128], tl["offdiag"][:],
                    reads=(KKZ, "offdiag"), writes=(KKZ,))
        for c in CX:
            for hh in range(2):
                h = 2 * c["p"] + hh
                reg = pb[0][:, (2 * c["q"] + hh) * 128:(2 * c["q"] + hh + 1) * 128]
                for s_i in range(3):
                    mm(reg, tl["hsel"][:, h * 128:(h + 1) * 128],
                       tl["bsp"][:, s_i * 128:(s_i + 1) * 128], s_i == 0, False,
                       reads=("hsel", "bsp"), writes=(PB[0],))
                mm(reg, tl["id_b"][:], tl["penL"][:], False, True,
                   reads=("id_b", "penL"), writes=(PB[0],))
        for c in CX:
            e, Ek = sl("E", c["q"], 256)
            for hh in range(2):
                h = 2 * c["p"] + hh
                reg = pb[0][:, (2 * c["q"] + hh) * 128:(2 * c["q"] + hh + 1) * 128]
                sact(e[:, hh * 128:(hh + 1) * 128], reg, AF.Exp,
                     tl["btm"][:, h:h + 1], -1.0, reads=(PB[0], "btm"), writes=(Ek,))
        for c in CX:
            q = c["q"]
            e, Ek = sl("E", q, 256)
            kkz, KKZ = sl("KKz", q, 256)
            kq, KQk = sl("KQ", q, 128)
            n0, N0k = sl("N0", q, 256)
            aa, Ak = sl("A", q, 256)
            for hh in range(2):
                vop("tensor_mul", n0[:, hh * 128:(hh + 1) * 128],
                    kkz[:, hh * 128:(hh + 1) * 128], e[:, hh * 128:(hh + 1) * 128],
                    reads=(KKZ, Ek), writes=(N0k,))
                vop("tensor_mul", aa[:, hh * 128:(hh + 1) * 128],
                    kq, e[:, hh * 128:(hh + 1) * 128],
                    reads=(KQk, Ek), writes=(Ak,))
        for c in CX:
            q = c["q"]
            n0, N0k = sl("N0", q, 256)
            aa, Ak = sl("A", q, 256)
            for hh in range(2):
                tpose(c["pq_"][:, hh * 128:(hh + 1) * 128], n0[:, hh * 128:(hh + 1) * 128],
                      tl["id_b"][:], reads=(N0k, "id_b"), writes=(c["PQ_"],))
                tpose(c["pq_"][:, 256 + hh * 128:256 + (hh + 1) * 128],
                      aa[:, hh * 128:(hh + 1) * 128], tl["id_b"][:],
                      reads=(Ak, "id_b"), writes=(c["PQ_"],))
            tpose(c["pq_"][:, 512:640], c["vpair"], tl["id_b"][:],
                  reads=("mixed", "id_b"), writes=(c["PQ_"],))
            tpose(c["pq_"][:, 640:704], c["krows"],
                  tl["id_b"][c["b"]:c["b"] + 64, c["b"]:c["b"] + 64],
                  reads=("mixed", "id_b"), writes=(c["PQ_"],))
        for c in CX:
            q = c["q"]
            nt, Ntk = sl("Nt", q, 256)
            at, Atk = sl("At", q, 256)
            vt, Vk = sl("Vtm", q, 128)
            kt, Kk = sl("Ktm", q, 64)
            evac(q, nt, c["pq_"][:, 0:256], reads=(c["PQ_"],), writes=(Ntk,))
            evac(q, at, c["pq_"][:, 256:512], reads=(c["PQ_"],), writes=(Atk,))
            evac(q, vt, c["pq_"][:, 512:640], reads=(c["PQ_"],), writes=(Vk,))
            evac(q, kt, c["pq_"][:, 640:704], reads=(c["PQ_"],), writes=(Kk,))
        for c in CX:
            q = c["q"]
            b = c["b"]
            for hh in range(2):
                h = 2 * c["p"] + hh
                mm(c["pm"][b:b + 64, 256 + hh * 128:256 + (hh + 1) * 128],
                   tl["hsel"][:, h * 128:h * 128 + 64], tl["ebqrowb"][:], True, True,
                   reads=("hsel", "ebqrowb"), writes=(c["PM"],), tp=(0, b))
            for hh in range(2):
                mm(c["pm"][:, hh * 64:(hh + 1) * 64], c["krows"],
                   tl["Sb"][b:b + 64, c["p"] * 128 + hh * 64: c["p"] * 128 + (hh + 1) * 64],
                   True, True, reads=("mixed", "Sb" + str(q)), writes=(c["PM"],))
        for c in CX:
            q = c["q"]
            b = c["b"]
            rp, Rk = tl["rpre"], "rpre"
            u, Uk = sl("U", q, 128)
            vt, Vk = sl("Vtm", q, 128)
            for hh in range(2):
                h = 2 * c["p"] + hh
                vop("tensor_mul", tl["Qt"][b:b + 64, hh * 128:(hh + 1) * 128], c["qrows"],
                    c["pm"][b:b + 64, 256 + hh * 128:256 + (hh + 1) * 128],
                    reads=("mixed", c["PM"]), writes=("Qt" + str(q),))
            for hh in range(2):
                h = 2 * c["p"] + hh
                vop("scalar_tensor_tensor", rp[:, hh * 64:(hh + 1) * 64],
                    c["pm"][:, hh * 64:(hh + 1) * 64], tl["negebtm"][:, h:h + 1],
                    vt[:, hh * 64:(hh + 1) * 64], ALU.mult, ALU.add,
                    reads=(c["PM"], "negebtm", Vk), writes=(Rk,))
                vop("tensor_scalar_mul", u[:, hh * 64:(hh + 1) * 64],
                    rp[:, hh * 64:(hh + 1) * 64], tl["betatm"][:, h:h + 1],
                    reads=(Rk, "betatm"), writes=(Uk,))
        papp = [pb[0], pb[1]]
        PAPP = [PB[0], PB[1]]
        for c in CX:
            q = c["q"]
            u, Uk = sl("U", q, 128)
            nt, Ntk = sl("Nt", q, 256)
            for hh in range(2):
                mm(papp[0][:, q * 128 + hh * 64:q * 128 + (hh + 1) * 64],
                   nt[:, hh * 128:(hh + 1) * 128], u[:, hh * 64:(hh + 1) * 64],
                   True, True, reads=(Ntk, Uk), writes=(PAPP[0],))
        for c in CX:
            q = c["q"]
            u, Uk = sl("U", q, 128)
            vop("tensor_add", u, u, papp[0][:, q * 128:(q + 1) * 128],
                reads=(Uk, PAPP[0]), writes=(Uk,))
        sqbank = {0: [pb[2], pb[3]], 1: [pb[4], pb[5]]}
        SQBANK = {0: [PB[2], PB[3]], 1: [PB[4], PB[5]]}
        for lv in range(1, 7):
            for c in CX:
                q = c["q"]
                bank = sqbank[q][lv % 2]
                bkey = SQBANK[q][lv % 2]
                n0, N0k = sl("N0", q, 256)
                nt, Ntk = sl("Nt", q, 256)
                if lv == 1:
                    lo = [(n0, N0k, slice(0, 128)), (n0, N0k, slice(128, 256))]
                    up = [(nt, Ntk, slice(0, 128)), (nt, Ntk, slice(128, 256))]
                else:
                    pt, pk = sl("sqA" if (lv - 1) % 2 else "sqB", q, 512)
                    lo = [(pt, pk, slice(0, 128)), (pt, pk, slice(256, 384))]
                    up = [(pt, pk, slice(128, 256)), (pt, pk, slice(384, 512))]
                for hh in range(2):
                    lt, lk, ls = lo[hh]
                    ut, uk, us = up[hh]
                    if lv < 6:
                        mm(bank[:, hh * 256:hh * 256 + 128], ut[:, us], lt[:, ls],
                           True, True, reads=(uk, lk), writes=(bkey,))
                    mm(bank[:, hh * 256 + 128:hh * 256 + 256], lt[:, ls], ut[:, us],
                       True, True, reads=(lk, uk), writes=(bkey,))
            for c in CX:
                q = c["q"]
                bank = sqbank[q][lv % 2]
                bkey = SQBANK[q][lv % 2]
                sq_t, sq_k = sl("sqA" if lv % 2 else "sqB", q, 512)
                if lv < 6:
                    evac(q, sq_t, bank[:], reads=(bkey,), writes=(sq_k,))
                else:
                    evac(q, sq_t[:, 128:256], bank[:, 128:256], reads=(bkey,), writes=(sq_k,))
                    evac(q, sq_t[:, 384:512], bank[:, 384:512], reads=(bkey,), writes=(sq_k,))
            for c in CX:
                q = c["q"]
                u, Uk = sl("U", q, 128)
                sq_t, sq_k = sl("sqA" if lv % 2 else "sqB", q, 512)
                for hh in range(2):
                    us = slice(128, 256) if hh == 0 else slice(384, 512)
                    mm(papp[lv % 2][:, q * 128 + hh * 64:q * 128 + (hh + 1) * 64],
                       sq_t[:, us], u[:, hh * 64:(hh + 1) * 64],
                       True, True, reads=(sq_k, Uk), writes=(PAPP[lv % 2],))
            for c in CX:
                q = c["q"]
                u, Uk = sl("U", q, 128)
                vop("tensor_add", u, u, papp[lv % 2][:, q * 128:(q + 1) * 128],
                    reads=(Uk, PAPP[lv % 2]), writes=(Uk,))
        for c in CX:
            q = c["q"]
            b = c["b"]
            u, Uk = sl("U", q, 128)
            at, Atk = sl("At", q, 256)
            for hh in range(2):
                mm(c["pm"][hh * 64:(hh + 1) * 64, 256:384], u[:, hh * 64:(hh + 1) * 64],
                   at[:, hh * 128:(hh + 1) * 128], True, False,
                   reads=(Uk, Atk), writes=(c["PM"],), tp=(0, hh * 64))
                mm(c["pm"][hh * 64:(hh + 1) * 64, 256:384],
                   tl["Sb"][b:b + 64, c["p"] * 128 + hh * 64: c["p"] * 128 + (hh + 1) * 64],
                   tl["Qt"][b:b + 64, hh * 128:(hh + 1) * 128],
                   False, True, reads=("Sb" + str(q), "Qt" + str(q)), writes=(c["PM"],),
                   tp=(b, hh * 64))
        for c in CX:
            q = c["q"]
            hp, Hk = sl("hpre", q, 128)
            vop("tensor_mul", hp, c["pm"][:, 256:384],
                tl["zs"][:, c["p"] * TT + col0: c["p"] * TT + col0 + C],
                reads=(c["PM"], "zs"), writes=(Hk,))
        for c in CX:
            q = c["q"]
            hp, Hk = sl("hpre", q, 128)
            hq, HQk = sl("hsq", q, 128)
            sch.op("s", (lambda eng, o=hq, i=hp:
                         eng.activation(o, i, AF.Square, bias=0.0, scale=1.0)),
                   reads=(Hk,), writes=(HQk,))
        for c in CX:
            q = c["q"]
            hq, HQk = sl("hsq", q, 128)
            mm(c["pm"][0:2, 384:512], tl["blockhalf"][:], hq, True, True,
               reads=("blockhalf", HQk), writes=(c["PM"],))
        for c in CX:
            q = c["q"]
            istd, Ik = sl("istd", q, 128)
            sact(istd, c["pm"][0:2, 384:512], AF.Ln, tl["eps_rms"][:], 1.0 / 64,
                 reads=(c["PM"], "eps_rms"), writes=(Ik,))
            sact(istd, istd, AF.Exp, 0.0, -0.5, reads=(Ik,), writes=(Ik,))
        for c in CX:
            q = c["q"]
            istd, Ik = sl("istd", q, 128)
            istdb, Ibk = sl("istdb", q, 128)
            vop("tensor_copy", istdb, istd, reads=(Ik,), writes=(Ibk,))
        for c in CX:
            q = c["q"]
            istdb, Ibk = sl("istdb", q, 128)
            mm(c["pm"][:, 384:512], tl["halfind"][:], istdb, True, True,
               reads=("halfind", Ibk), writes=(c["PM"],))
        for c in CX:
            q = c["q"]
            hp, Hk = sl("hpre", q, 128)
            vop("scalar_tensor_tensor", tl["hb"][:, c["p"] * TT + col0: c["p"] * TT + col0 + C],
                hp, tl["nwdup"][:], c["pm"][:, 384:512], ALU.mult, ALU.mult,
                reads=(Hk, "nwdup", c["PM"]), writes=("hb",))
        for c in CX:
            q = c["q"]
            u, Uk = sl("U", q, 128)
            for hh in range(2):
                h = 2 * c["p"] + hh
                vop("tensor_scalar_mul", u[:, hh * 64:(hh + 1) * 64],
                    u[:, hh * 64:(hh + 1) * 64], tl["ebcmbtm"][:, h:h + 1],
                    reads=(Uk, "ebcmbtm"), writes=(Uk,))
        for c in CX:
            q = c["q"]
            b = c["b"]
            u, Uk = sl("U", q, 128)
            kt, Kk = sl("Ktm", q, 64)
            for hh in range(2):
                mm(c["pm"][b:b + 64, hh * 64:(hh + 1) * 64], kt,
                   u[:, hh * 64:(hh + 1) * 64], True, True,
                   reads=(Kk, Uk), writes=(c["PM"],), tp=(0, b))
        for c in CX:
            q = c["q"]
            b = c["b"]
            for hh in range(2):
                h = 2 * c["p"] + hh
                col = c["p"] * 128 + hh * 64
                vop("scalar_tensor_tensor", tl["S"][b:b + 64, col:col + 64],
                    tl["S"][b:b + 64, col:col + 64], tl["ebc"][b:b + 64, h:h + 1],
                    c["pm"][b:b + 64, hh * 64:(hh + 1) * 64], ALU.mult, ALU.add,
                    reads=("S" + str(q), "ebc", c["PM"]), writes=("S" + str(q),))
                evac(q, tl["Sb"][b:b + 64, col:col + 64], tl["S"][b:b + 64, col:col + 64],
                     reads=("S" + str(q),), writes=("Sb" + str(q),))

    def layer_norm(dest):
        for kc in range(8):
            vop("tensor_copy", tl["xb"][:, kc * TT:(kc + 1) * TT],
                tl["x32"][:, kc * TT:(kc + 1) * TT], reads=("x32",), writes=("xb",))
        for kc in range(8):
            sch.op("s", (lambda eng, o=tl["hb"][:, kc * TT:(kc + 1) * TT],
                         i=tl["xb"][:, kc * TT:(kc + 1) * TT]:
                         eng.activation(o, i, AF.Square, bias=0.0, scale=1.0)),
                   reads=("xb",), writes=("hb",))
        for kc in range(8):
            mm(pb[2][0:1, 0:TT], tl["onescol"][:], tl["xb"][:, kc * TT:(kc + 1) * TT],
               kc == 0, kc == 7, reads=("xb", "onescol"), writes=(PB[2],))
        for kc in range(8):
            mm(pb[2][0:1, 256:256 + TT], tl["onescol"][:], tl["hb"][:, kc * TT:(kc + 1) * TT],
               kc == 0, kc == 7, reads=("hb", "onescol"), writes=(PB[2],))
        vop("tensor_scalar_mul", tl["lnrow"][:, 0:TT], pb[2][0:1, 0:TT], 1.0 / D,
            reads=(PB[2],), writes=("lnrow",))
        vop("tensor_scalar_mul", tl["lnrow"][:, TT:2 * TT], pb[2][0:1, 256:256 + TT], 1.0 / D,
            reads=(PB[2],), writes=("lnrow",))
        vop("tensor_copy", tl["lnb"][:, 0:TT], tl["lnrow"][:, 0:TT],
            reads=("lnrow",), writes=("lnb",))
        vop("tensor_mul", tl["lnrow"][:, 0:TT], tl["lnrow"][:, 0:TT],
            tl["lnrow"][:, 0:TT], reads=("lnrow",), writes=("lnrow",))
        vop("tensor_sub", tl["lnrow"][:, TT:2 * TT], tl["lnrow"][:, TT:2 * TT],
            tl["lnrow"][:, 0:TT], reads=("lnrow",), writes=("lnrow",))
        sact(tl["lnrow"][:, TT:2 * TT], tl["lnrow"][:, TT:2 * TT], AF.Ln,
             tl["eps_ln"][:], 1.0, reads=("lnrow", "eps_ln"), writes=("lnrow",))
        sact(tl["lnrow"][:, TT:2 * TT], tl["lnrow"][:, TT:2 * TT], AF.Exp,
             0.0, -0.5, reads=("lnrow",), writes=("lnrow",))
        vop("tensor_copy", tl["lnb"][:, TT:2 * TT], tl["lnrow"][:, TT:2 * TT],
            reads=("lnrow",), writes=("lnb",))

        mm(pb[3][:, 0:TT], tl["onesrow"][:], tl["lnb"][:, 0:TT], True, True,
           reads=("lnb", "onesrow"), writes=(PB[3],))
        mm(pb[3][:, 256:256 + TT], tl["onesrow"][:], tl["lnb"][:, TT:2 * TT], True, True,
           reads=("lnb", "onesrow"), writes=(PB[3],))
        for kc in range(8):
            vop("tensor_sub", tl["ftmp"][:], tl["x32"][:, kc * TT:(kc + 1) * TT],
                pb[3][:, 0:TT], reads=("x32", PB[3]), writes=("ftmp",))
            vop("tensor_mul", tl["x32"][:, kc * TT:(kc + 1) * TT], tl["ftmp"][:],
                pb[3][:, 256:256 + TT], reads=("ftmp", PB[3]), writes=("x32",))
        if dest is not None:
            for kc in range(8):
                scopy(tl[dest][:, kc * TT:(kc + 1) * TT], tl["x32"][:, kc * TT:(kc + 1) * TT],
                      reads=("x32",), writes=("hb" if dest == "xq" else dest,))

    def emit_tile(t):
        x_sl = dr["xT"][:, t * TT:(t + 1) * TT].rearrange("(c p) j -> p c j", p=128)
        dma("x", tl["x32"][:].rearrange("p (c j) -> p c j", c=8), x_sl,
            reads=("x32",), writes=("x32",))
        for kc in range(8):
            vop("tensor_copy", tl["xb"][:, kc * TT:(kc + 1) * TT],
                tl["x32"][:, kc * TT:(kc + 1) * TT], reads=("x32",), writes=("xb",))

        if DBG_STAGE == "prol":
            return

        def dense(oc, nkc, wt, wkey, stride, bank):
            for kc in range(nkc):
                mm(pb[bank][:, 0:TT], tl[wt][:, kc * stride + oc * 128: kc * stride + (oc + 1) * 128],
                   tl["xb" if wt in ("wqkv", "wz", "wba") else
                      ("hb" if wt == "wout" else ("xq" if wt == "w1" else "h1"))]
                   [:, kc * TT:(kc + 1) * TT],
                   kc == 0, kc == nkc - 1,
                   reads=(wkey, "xb" if wt in ("wqkv", "wz", "wba") else
                          ("hb" if wt in ("wout", "w1") else "h1")),
                   writes=(PB[bank],))

        for oc in range(16):
            bank = oc % 2
            dense(oc, 8, "wqkv", "wqkv", CONV, bank)
            scopy(mxc(oc, 3, 3 + TT), pb[bank][:, 0:TT], reads=(PB[bank],), writes=("mixed",))
        for oc in range(8):
            bank = oc % 2
            dense(oc, 8, "wz", "wz", VAL, bank)
            sact(tl["zs"][:, oc * TT:(oc + 1) * TT], pb[bank][:, 0:TT], AF.Silu, 0.0, 1.0,
                 reads=(PB[bank],), writes=("zs",))
        for kc in range(8):
            mm(pb[2][0:48, 0:TT], tl["wba"][:, kc * 48:(kc + 1) * 48],
               tl["xb"][:, kc * TT:(kc + 1) * TT], kc == 0, kc == 7,
               reads=("xb", "wba"), writes=(PB[2],))
        vop("tensor_copy", tl["bat"][:, 0:TT], pb[2][0:48, 0:TT],
            reads=(PB[2],), writes=("bat",))

        if DBG_STAGE == "proj":
            return
        for c in range(16):
            cw = lambda j: tl["convw"][:, c * 4 + j: c * 4 + j + 1]
            vop("tensor_scalar_mul", tl["cvt"][:], mxc(c, 0, TT), cw(0),
                reads=("mixed", "convw"), writes=("cvt",))
            for j in (1, 2, 3):
                vop("scalar_tensor_tensor", tl["cvt"][:], mxc(c, j, j + TT), cw(j),
                    tl["cvt"][:], ALU.mult, ALU.add,
                    reads=("mixed", "convw", "cvt"), writes=("cvt",))
            vop("tensor_copy", mxc(c, 0, 3), mxc(c, TT, TT + 3),
                reads=("mixed",), writes=("mixed",))
            sact(mxc(c, 3, 3 + TT), tl["cvt"][:], AF.Silu, 0.0, 1.0,
                 reads=("cvt",), writes=("mixed",))

        if DBG_STAGE == "conv":
            return
        sact(tl["gsc"][:, 0:TT], tl["bat"][0:16, 0:TT], AF.Exp, tl["dtb16"][:], 1.0,
             reads=("bat", "dtb16"), writes=("gsc",))
        vop("tensor_scalar_add", tl["gsc"][:, 0:TT], tl["gsc"][:, 0:TT], 1.0,
            reads=("gsc",), writes=("gsc",))
        sact(tl["ga"][:, 0:TT], tl["gsc"][:, 0:TT], AF.Ln, 0.0, 1.0,
             reads=("gsc",), writes=("ga",))
        vop("tensor_scalar_mul", tl["ga"][:, 0:TT], tl["ga"][:, 0:TT], tl["negA"][:],
            reads=("ga", "negA"), writes=("ga",))

        if DBG_STAGE == "ga":
            return
        for cc in range(NCH if DBG_CHUNKS is None else DBG_CHUNKS):
            emit_chunk(t, cc)
        if DBG_SCAN_ONLY or DBG_STAGE == "scan":
            return

        for oc in range(8):
            bank = oc % 2
            dense(oc, 8, "wout", "wout", D, bank)
            vop("tensor_add", tl["x32"][:, oc * TT:(oc + 1) * TT], pb[bank][:, 0:TT],
                tl["x32"][:, oc * TT:(oc + 1) * TT], reads=(PB[bank], "x32"), writes=("x32",))
        layer_norm("xq")
        if DBG_STOP_LN1:
            return

        for oc1 in range(32):
            bank = oc1 % 2
            dense(oc1, 8, "w1", "w1", H1, bank)
            sact(tl["f16t"][:], pb[bank][:, 0:TT], AF.Relu, tl["b1q"][:, oc1:oc1 + 1], 1.0,
                 reads=(PB[bank], "b1q"), writes=("f16t",))
            vop("tensor_copy", tl["h1"][:, oc1 * TT:(oc1 + 1) * TT], tl["f16t"][:],
                reads=("f16t",), writes=("h1",))
        for oc2 in range(8):
            bank = oc2 % 2
            dense(oc2, 32, "w2", "w2", D, bank)
            sact(tl["f16t"][:], pb[bank][:, 0:TT], AF.Identity, tl["b2q"][:, oc2:oc2 + 1], 1.0,
                 reads=(PB[bank], "b2q"), writes=("f16t",))
            if oc2 % 4 == 0:
                mh_sl = dr["maskT"][oc2 * 128:(oc2 + 4) * 128, t * TT:(t + 1) * TT] \
                    .rearrange("(c p) j -> p c j", p=128)
                dma("m", tl["msk"][:].rearrange("p (c j) -> p c j", c=4), mh_sl,
                    reads=("msk",), writes=("msk",), queue="g")
            vop("tensor_mul", tl["ftmp"][:], tl["f16t"][:], tl["msk"][:, (oc2 % 4) * TT:(oc2 % 4 + 1) * TT],
                reads=("f16t", "msk"), writes=("ftmp",))
            vop("tensor_add", tl["x32"][:, oc2 * TT:(oc2 + 1) * TT], tl["ftmp"][:],
                tl["x32"][:, oc2 * TT:(oc2 + 1) * TT], reads=("ftmp", "x32"), writes=("x32",))
        layer_norm(None)
        o_sl = dr["outT"][:, t * TT:(t + 1) * TT].rearrange("(c p) j -> p c j", p=128)
        dma("o", o_sl, tl["x32"][:].rearrange("p (c j) -> p c j", c=8),
            reads=("x32",), writes=())

    for t in range(N_TILES_EMIT):
        emit_tile(t)

    sch.finalize()
    with ctx:
        sch.emit(nc, ctx)
    return nc


_NC_CACHE = {}


def _build_cached():
    key = (N_TILES_EMIT, DEBUG_TAPS)
    if key not in _NC_CACHE:
        _NC_CACHE[key] = build_program()
    return _NC_CACHE[key]


def kernel(L0_attn_hidden_states, L0_attn_W_qkv, L0_attn_conv_w, L0_attn_W_z,
           L0_attn_W_b, L0_attn_W_a, L0_attn_dt_bias, L0_attn_A_log,
           L0_attn_norm_weight, L0_attn_W_out, L0_ffn_w1, L0_ffn_b1,
           L0_ffn_w2, L0_ffn_b2):
    import jax

    x = np.asarray(L0_attn_hidden_states, dtype=np.float32)
    wqkv = np.asarray(L0_attn_W_qkv, dtype=np.float32)
    convw = np.asarray(L0_attn_conv_w, dtype=np.float32)
    wz = np.asarray(L0_attn_W_z, dtype=np.float32)
    wb = np.asarray(L0_attn_W_b, dtype=np.float32)
    wa = np.asarray(L0_attn_W_a, dtype=np.float32)
    dtb = np.asarray(L0_attn_dt_bias, dtype=np.float32)
    alog = np.asarray(L0_attn_A_log, dtype=np.float32)
    nw = np.asarray(L0_attn_norm_weight, dtype=np.float32)
    wout = np.asarray(L0_attn_W_out, dtype=np.float32)
    w1 = np.asarray(L0_ffn_w1, dtype=np.float32)
    b1 = np.asarray(L0_ffn_b1, dtype=np.float32)
    w2 = np.asarray(L0_ffn_w2, dtype=np.float32)
    b2 = np.asarray(L0_ffn_b2, dtype=np.float32)

    with jax.default_device(jax.devices("cpu")[0]):
        keep = np.asarray(jax.random.bernoulli(jax.random.key(42, impl="threefry2x32"), 0.8, x.shape))
    maskv = np.where(keep, np.float32(1.25), np.float32(0.0))

    wba = np.zeros((48, D), np.float32)
    wba[0:16] = wa
    wba[32:48] = wb
    shared = {
        "wqkvT": np.ascontiguousarray(wqkv.T),
        "wzT": np.ascontiguousarray(wz.T),
        "wbaT": np.ascontiguousarray(wba.T),
        "woutT": np.ascontiguousarray(wout.T),
        "w1T": np.ascontiguousarray(w1.T),
        "w2T": np.ascontiguousarray(w2.T),
        "convw": np.ascontiguousarray(
            convw[:, 0, :].reshape(16, 128, 4).transpose(1, 0, 2).reshape(128, 64)),
        "b1r": np.ascontiguousarray(b1.reshape(32, 128).T),
        "b2r": np.ascontiguousarray(b2.reshape(8, 128).T),
        "alog16": np.ascontiguousarray(alog.reshape(16, 1)),
        "dtb16": np.ascontiguousarray(dtb.reshape(16, 1)),
        "nwdup": np.ascontiguousarray(np.tile(nw, 2).reshape(128, 1)),
    }
    in_maps = []
    for c in range(NCORE):
        m = dict(shared)
        m["xT"] = np.ascontiguousarray(x[c].T)
        m["maskT"] = np.ascontiguousarray(maskv[c].T.astype(np.float32))
        in_maps.append(m)

    nc = _build_cached()
    res = run_bass_kernel_spmd(nc, in_maps, core_ids=list(range(NCORE)))
    out = np.empty((NCORE, T, D), dtype=np.float32)
    for c in range(NCORE):
        out[c] = res.results[c]["outT"].T
    return out


# version 03 (final)
# speedup vs baseline: 2.0779x; 1.0174x over previous
import math
import numpy as np
import ml_dtypes
from contextlib import ExitStack

import concourse.bass as bass
import concourse.mybir as mybir
from concourse.bass_utils import run_bass_kernel_spmd

F32 = mybir.dt.float32
BF16 = mybir.dt.bfloat16
FP16 = mybir.dt.float16
FP8 = mybir.dt.float8e4
U16 = mybir.dt.uint16
AF = mybir.ActivationFunctionType
ALU = mybir.AluOpType

NCORE = 8
T = 4096
D = 1024
TT = 256
NT = T // TT
C = 128
NCH = TT // C
CONV = 2048
VAL = 1024
NPAIR = 8
H1 = 4096
PEN = 1.0e5
EPS_LN = 1e-5
EPS_RMS = 1e-6

N_TILES_EMIT = NT
DBG_SKIP = set()
DBG_PSTEP = 99
DBG_SCAN_ONLY = False
DBG_STOP_LN1 = False
DBG_STAGE = None
DBG_CHUNKS = None
DBG_PAIRS = None
DEBUG_TAPS = ()


class Op:
    __slots__ = ("eng", "fn", "need_inc", "sem", "amt", "waits", "idx_hint",
                 "cum", "note")

    def __init__(self, eng, fn, sem, amt, note=""):
        self.eng = eng
        self.fn = fn
        self.need_inc = False
        self.sem = sem
        self.amt = amt
        self.waits = {}
        self.note = note


class Sched:
    ENGINES = ("q", "g", "v", "s", "t")

    def __init__(self):
        self.ops = []
        self.res = {}
        self.dma_seq = {}
        self.eng_last = {}

    def _edge(self, prod, cons):
        if prod is None or prod is cons:
            return
        if prod.eng == cons.eng and prod.eng == "t":
            return
        prod.need_inc = True
        cur = cons.waits.get(prod.sem)
        if cur is None or cur.idx_hint < prod.idx_hint:
            cons.waits[prod.sem] = prod

    def _track(self, o, reads, writes):
        o.idx_hint = len(self.ops)
        self.ops.append(o)
        for k in reads:
            st = self.res.setdefault(k, {"w": None, "r": []})
            self._edge(st["w"], o)
            if k.startswith("PB") or k.startswith("PQ"):
                for r in st["r"]:
                    if r.eng != o.eng:
                        self._edge(r, o)
            st["r"].append(o)
        for k in writes:
            st = self.res.setdefault(k, {"w": None, "r": []})
            self._edge(st["w"], o)
            for r in st["r"]:
                self._edge(r, o)
            st["w"] = o
            st["r"] = []
        self.eng_last[o.eng] = o
        return o

    def op(self, eng, fn, reads=(), writes=(), note=""):
        return self._track(Op(eng, fn, "sem_" + eng, 1,
                              note or getattr(self, "cur_note", "")), reads, writes)

    def dma(self, stream, fn, reads=(), writes=(), note="", queue="q"):
        prev = self.dma_seq.get(stream)
        o = Op(queue, fn, f"dma_{stream}", 16, note)
        o.need_inc = True
        self._track(o, reads, writes)
        if prev is not None:
            o.waits[o.sem] = prev
        self.dma_seq[stream] = o
        return o

    def finalize(self):
        counts = {}
        for o in self.ops:
            if o.need_inc:
                counts[o.sem] = counts.get(o.sem, 0) + o.amt
                o.cum = counts[o.sem]
        self.sem_totals = counts

    def emit(self, nc, ctx):
        sems = {}
        names = sorted({o.sem for o in self.ops if o.need_inc}
                       | {o2.sem for o in self.ops for o2 in o.waits.values()})
        for n in names:
            sems[n] = ctx.enter_context(nc.semaphore(n))
        per_eng = {e: [] for e in self.ENGINES}
        for o in self.ops:
            per_eng[o.eng].append(o)
        final_waits = [(sems[n], tot) for n, tot in sorted(self.sem_totals.items())
                       if n.startswith("dma_o_")]

        def mk_body(eng_name, add_drain):
            ops = per_eng[eng_name]

            def body(eng):
                hw = {}
                for o in ops:
                    for sem_name in sorted(o.waits):
                        v = o.waits[sem_name].cum
                        if hw.get(sem_name, 0) < v:
                            eng.wait_ge(sems[sem_name], v)
                            hw[sem_name] = v
                    ins = o.fn(eng)
                    if o.need_inc:
                        ins.then_inc(sems[o.sem], o.amt)
                if add_drain:
                    for s, tot in final_waits:
                        eng.wait_ge(s, tot)
            return body

        with nc.Block() as block:
            block.sync(mk_body("q", False))
            if per_eng["g"]:
                block.gpsimd(mk_body("g", False))
            block.vector(mk_body("v", False))
            block.scalar(mk_body("s", False))
            block.tensor(mk_body("t", True))


def build_program():
    nc = bass.Bass("TRN2", target_bir_lowering=False)
    sch = Sched()

    dr = {}
    dr["xT"] = nc.dram_tensor("xT", [D, T], F32, kind="ExternalInput").ap()
    dr["maskT"] = nc.dram_tensor("maskT", [D, T], F32, kind="ExternalInput").ap()
    dr["wqkvT"] = nc.dram_tensor("wqkvT", [D, CONV], F32, kind="ExternalInput").ap()
    dr["wzT"] = nc.dram_tensor("wzT", [D, VAL], F32, kind="ExternalInput").ap()
    dr["wbaT"] = nc.dram_tensor("wbaT", [D, 48], F32, kind="ExternalInput").ap()
    dr["woutT"] = nc.dram_tensor("woutT", [VAL, D], F32, kind="ExternalInput").ap()
    dr["w1T"] = nc.dram_tensor("w1T", [D, H1], F32, kind="ExternalInput").ap()
    dr["w2T"] = nc.dram_tensor("w2T", [H1, D], F32, kind="ExternalInput").ap()
    dr["convw"] = nc.dram_tensor("convw", [128, 64], F32, kind="ExternalInput").ap()
    dr["b1r"] = nc.dram_tensor("b1r", [128, 32], F32, kind="ExternalInput").ap()
    dr["b2r"] = nc.dram_tensor("b2r", [128, 8], F32, kind="ExternalInput").ap()
    dr["alog16"] = nc.dram_tensor("alog16", [16, 1], F32, kind="ExternalInput").ap()
    dr["dtb16"] = nc.dram_tensor("dtb16", [16, 1], F32, kind="ExternalInput").ap()
    dr["nwdup"] = nc.dram_tensor("nwdup", [128, 1], F32, kind="ExternalInput").ap()
    dr["outT"] = nc.dram_tensor("outT", [D, T], F32, kind="ExternalOutput").ap()
    taps = {}
    for name in DEBUG_TAPS:
        taps[name] = nc.dram_tensor("tap_" + name, [128, 8192], F32,
                                    kind="ExternalOutput").ap()

    def bf(x):
        return np.asarray(x, dtype=ml_dtypes.bfloat16).view(np.uint16)

    ji, ii = np.meshgrid(np.arange(128), np.arange(128), indexing="ij")
    cn = {}
    cn["id_b"] = nc.inline_tensor(bf(np.eye(128)), name="c0")
    cn["id_f"] = nc.inline_tensor(np.eye(128, dtype=np.float32), name="c1")
    cn["utones"] = nc.inline_tensor(bf((ji <= ii).astype(np.float32)), name="c2")
    cn["penL"] = nc.inline_tensor(bf(np.where(ii > ji, PEN, 0.0)), name="c3")
    cn["offdiag"] = nc.inline_tensor(bf(1.0 - np.eye(128)), name="c4")
    cn["onesrow"] = nc.inline_tensor(bf(np.ones((1, 128))), name="c5")
    cn["onescol"] = nc.inline_tensor(bf(np.ones((128, 1))), name="c6")
    hind = np.zeros((2, 128), np.float32)
    hind[0, 0:64] = 1.0
    hind[1, 64:128] = 1.0
    cn["halfind"] = nc.inline_tensor(bf(hind), name="c7")
    bh = np.zeros((128, 2), np.float32)
    bh[0:64, 0] = 1.0
    bh[64:128, 1] = 1.0
    cn["blockhalf"] = nc.inline_tensor(bf(bh), name="c8")
    hs = np.zeros((16, 16 * 128), np.float32)
    for h in range(16):
        hs[h, h * 128:(h + 1) * 128] = 1.0
    cn["hsel"] = nc.inline_tensor(bf(hs), name="c9")

    ctx = ExitStack()
    sb = lambda n, shp, dt: ctx.enter_context(nc.sbuf_tensor(n, shp, dt))
    ps = lambda n, shp, dt=F32: ctx.enter_context(nc.psum_tensor(n, shp, dt))

    tl = {}
    for k, shp, dt in [
        ("id_b", [128, 128], BF16), ("id_f", [128, 128], F32),
        ("utones", [128, 128], BF16), ("penL", [128, 128], BF16),
        ("offdiag", [128, 128], BF16), ("onesrow", [1, 128], BF16),
        ("onescol", [128, 1], BF16), ("halfind", [2, 128], BF16),
        ("blockhalf", [128, 2], BF16), ("hsel", [16, 16 * 128], BF16),
        ("convw", [128, 64], F32), ("b1r", [128, 32], F32), ("b2r", [128, 8], F32),
        ("alog16", [16, 1], F32), ("dtb16", [16, 1], F32), ("negA", [16, 1], F32),
        ("nwdup", [128, 1], F32), ("eps_ln", [1, 1], F32), ("eps_rms", [2, 1], F32),
        ("ln8", [16, 1], F32),
        ("wqkv", [128, 8 * CONV], BF16), ("wz", [128, 8 * VAL], BF16),
        ("wba", [128, 8 * 48], BF16), ("wout", [128, 8 * D], BF16),
        ("w1", [128, 8 * H1], FP8), ("w2", [128, 32 * D], FP8),
        ("b1q", [128, 32], F32), ("b2q", [128, 8], F32), ("bq8", [128, 32], FP8),
        ("x32", [128, 8 * TT], F32), ("xb", [128, 8 * TT], BF16),
        ("mixed", [128, 16 * (TT + 3)], BF16), ("zs", [128, 8 * TT], BF16),
        ("bat", [48, TT], F32), ("ga", [16, TT], F32), ("gsc", [16, TT], F32),
        ("cvt", [128, TT], F32), ("hb", [128, 8 * TT], BF16),
        ("h1", [128, 32 * TT], FP8),
        ("f16t", [128, TT], FP16), ("ftmp", [128, TT], F32),
        ("msk", [128, 4 * TT], BF16),
        ("lnrow", [1, 2 * TT], F32), ("lnb", [1, 2 * TT], BF16),
        ("S", [128, NPAIR * 128], F32), ("Sb", [128, NPAIR * 128], BF16),
        ("btm", [128, 16], F32), ("gtmh", [128, 16], BF16), ("gtml", [128, 16], BF16),
        ("gtmp", [128, 16], F32), ("brows", [16, 128], F32), ("bsp", [16, 3 * 128], BF16),
        ("bsc", [16, 2 * 128], F32),
        ("betatm", [128, 16], F32), ("negebtm", [128, 16], F32),
        ("ebtm", [128, 16], F32), ("ebcmbfm", [16, 128], F32), ("ebcmbtm", [128, 16], F32),
        ("ebqrowb", [16, 128], BF16), ("bcrow", [1, 16], F32), ("ebcrow", [1, 16], F32),
        ("ebcrowb", [1, 2 * 16], BF16), ("ebc", [128, 16], F32),
        ("E", [128, 512], BF16), ("KKz", [128, 512], BF16), ("KQ", [128, 256], BF16),
        ("A", [128, 512], BF16), ("At", [128, 512], BF16),
        ("N0", [128, 512], BF16), ("Nt", [128, 512], BF16),
        ("sqA", [128, 1024], BF16), ("sqB", [128, 1024], BF16),
        ("Vtm", [128, 256], BF16), ("Ktm", [128, 128], BF16), ("Qt", [128, 256], BF16),
        ("rpre", [128, 128], F32), ("U", [128, 256], BF16),
        ("hpre", [128, 256], F32), ("hsq", [128, 256], BF16),
        ("istd", [2, 256], F32), ("istdb", [2, 256], BF16),
    ]:
        tl[k] = sb("t_" + k, shp, dt)
    tl["xq"] = nc.alloc_sbuf_tensor_at(
        "t_xq", [128, 8 * TT], FP8, offset=nc.lookup_mloc(tl["hb"]).addr)
    tl["stg"] = nc.alloc_sbuf_tensor_at(
        "t_stg", [128, 512], F32, offset=nc.lookup_mloc(tl["h1"]).addr)

    pb = [ps(f"pb{i}", [128, 512], F32) for i in range(6)]
    pq = [ps(f"pq{i}", [128, 1024], BF16) for i in range(2)]
    PB = [f"PB{i}" for i in range(6)]
    PQ = [f"PQ{i}" for i in range(2)]

    def mm(out_ap, lhsT_ap, rhs_ap, start, stop, reads, writes, tp=None):
        def fn(eng, o=out_ap, l=lhsT_ap, r=rhs_ap, st=start, sp=stop, tpos=tp):
            return eng.matmul(o, l, r, start=st, stop=sp, tile_position=tpos)
        return sch.op("t", fn, reads=reads, writes=writes)

    def tpose(out_ap, in_ap, ident_ap, reads, writes):
        def fn(eng, o=out_ap, i=in_ap, d=ident_ap):
            return eng.transpose(o, i, d)
        return sch.op("t", fn, reads=reads, writes=writes)

    def vop(name, *args, reads=(), writes=()):
        def fn(eng, n=name, a=args):
            return getattr(eng, n)(*a)
        return sch.op("v", fn, reads=reads, writes=writes)

    def sact(out_ap, in_ap, func, bias, scale, reads, writes):
        def fn(eng, o=out_ap, i=in_ap, f=func, b=bias, s=scale):
            return eng.activation(o, i, f, bias=b, scale=s)
        return sch.op("s", fn, reads=reads, writes=writes)

    def scopy(out_ap, in_ap, reads, writes):
        def fn(eng, o=out_ap, i=in_ap):
            return eng.copy(o, i)
        return sch.op("s", fn, reads=reads, writes=writes)

    def dma(stream, out_ap, in_ap, reads, writes, queue="q"):
        def fn(eng, o=out_ap, i=in_ap):
            return eng.dma_start(out=o, in_=i)
        return sch.dma(stream, fn, reads=reads, writes=writes, queue=queue)

    def mxc(c, a, b2_):
        return tl["mixed"][:, c * (TT + 3) + a: c * (TT + 3) + b2_]

    for k in ("id_b", "utones", "penL", "offdiag", "onesrow", "onescol",
              "halfind", "blockhalf", "hsel"):
        dma("c", tl[k][:], cn[k].ap().bitcast(BF16), reads=(), writes=(k,))
    dma("c", tl["id_f"][:], cn["id_f"].ap(), reads=(), writes=("id_f",))
    for k in ("convw", "b1r", "b2r", "alog16", "dtb16", "nwdup"):
        dma("c", tl[k][:], dr[k], reads=(), writes=(k,))

    vop("memset", tl["eps_ln"][:], EPS_LN, writes=("eps_ln",))
    vop("memset", tl["eps_rms"][:], EPS_RMS, writes=("eps_rms",))
    vop("memset", tl["ln8"][:], math.log(0.125), writes=("ln8",))
    vop("memset", tl["S"][:], 0.0, writes=("S0", "S1"))
    vop("memset", tl["Sb"][:], 0.0, writes=("Sb0", "Sb1"))
    for c in range(16):
        vop("memset", mxc(c, 0, 3), 0.0, writes=("mixed",))
    sact(tl["negA"][:], tl["alog16"][:], AF.Exp, 0.0, 1.0,
         reads=("alog16",), writes=("negA",))
    vop("tensor_scalar_mul", tl["negA"][:], tl["negA"][:], -1.0,
        reads=("negA",), writes=("negA",))

    def stage_cast_big(dst_tile, dst_off, src_ap2d, wkey, cols):
        for half in range(0, cols, 512):
            w = min(512, cols - half)
            dma("w", tl["stg"][:, 0:w], src_ap2d[:, half:half + w],
                reads=("h1",), writes=("h1",))
            vop("tensor_copy", dst_tile[:, dst_off + half: dst_off + half + w],
                tl["stg"][:, 0:w], reads=("h1",), writes=(wkey,))

    for kc in range(8):
        stage_cast_big(tl["wqkv"], kc * CONV,
                       dr["wqkvT"][kc * 128:(kc + 1) * 128, :], "wqkv", CONV)
        stage_cast_big(tl["wz"], kc * VAL,
                       dr["wzT"][kc * 128:(kc + 1) * 128, :], "wz", VAL)
        stage_cast_big(tl["wba"], kc * 48,
                       dr["wbaT"][kc * 128:(kc + 1) * 128, :], "wba", 48)
        stage_cast_big(tl["wout"], kc * D,
                       dr["woutT"][kc * 128:(kc + 1) * 128, :], "wout", D)
        stage_cast_big(tl["w1"], kc * H1,
                       dr["w1T"][kc * 128:(kc + 1) * 128, :], "w1", H1)
        for sub in range(4):
            kc2 = kc * 4 + sub
            stage_cast_big(tl["w2"], kc2 * D,
                           dr["w2T"][kc2 * 128:(kc2 + 1) * 128, :], "w2", D)
    vop("tensor_copy", tl["bq8"][:, 0:32], tl["b1r"][:], reads=("b1r",), writes=("bq8",))
    vop("tensor_copy", tl["b1q"][:], tl["bq8"][:, 0:32], reads=("bq8",), writes=("b1q",))
    vop("tensor_copy", tl["bq8"][:, 0:8], tl["b2r"][:], reads=("b2r",), writes=("bq8",))
    vop("tensor_copy", tl["b2q"][:], tl["bq8"][:, 0:8], reads=("bq8",), writes=("b2q",))

    def emit_chunk(t, cc):
        col0 = cc * C
        sch.cur_note = "g-cumsum"
        idf16 = tl["id_f"][0:16, 0:16]
        tpose(pb[4][0:128, 0:16], tl["ga"][:, col0:col0 + C], idf16,
              reads=("ga", "id_f"), writes=(PB[4],))
        vop("tensor_copy", tl["gtmh"][:], pb[4][0:128, 0:16], reads=(PB[4],), writes=("gtmh",))
        vop("tensor_sub", tl["gtmp"][:], pb[4][0:128, 0:16], tl["gtmh"][:],
            reads=(PB[4], "gtmh"), writes=("gtmp",))
        vop("tensor_copy", tl["gtml"][:], tl["gtmp"][:], reads=("gtmp",), writes=("gtml",))
        mm(pb[5][0:128, 0:16], tl["utones"][:], tl["gtmh"][:], True, False,
           reads=("gtmh", "utones"), writes=(PB[5],))
        mm(pb[5][0:128, 0:16], tl["utones"][:], tl["gtml"][:], False, True,
           reads=("gtml", "utones"), writes=(PB[5],))
        vop("tensor_copy", tl["btm"][:], pb[5][0:128, 0:16], reads=(PB[5],), writes=("btm",))
        tpose(pb[4][0:16, 16:144], tl["btm"][:], tl["id_f"][:],
              reads=("btm", "id_f"), writes=(PB[4],))
        vop("tensor_copy", tl["brows"][:], pb[4][0:16, 16:144], reads=(PB[4],), writes=("brows",))
        vop("tensor_copy", tl["bsp"][:, 0:128], tl["brows"][:], reads=("brows",), writes=("bsp",))
        vop("tensor_sub", tl["bsc"][:, 0:128], tl["brows"][:], tl["bsp"][:, 0:128],
            reads=("brows", "bsp"), writes=("bsc",))
        vop("tensor_copy", tl["bsp"][:, 128:256], tl["bsc"][:, 0:128],
            reads=("bsc",), writes=("bsp",))
        vop("tensor_sub", tl["bsc"][:, 128:256], tl["bsc"][:, 0:128], tl["bsp"][:, 128:256],
            reads=("bsc", "bsp"), writes=("bsc",))
        vop("tensor_copy", tl["bsp"][:, 256:384], tl["bsc"][:, 128:256],
            reads=("bsc",), writes=("bsp",))
        tpose(pb[5][0:128, 16:32], tl["bat"][32:48, col0:col0 + C],
              tl["id_f"][32:48, 32:48], reads=("bat", "id_f"), writes=(PB[5],))
        sact(tl["betatm"][:], pb[5][0:128, 16:32], AF.Sigmoid, 0.0, 1.0,
             reads=(PB[5],), writes=("betatm",))
        sact(tl["ebtm"][:], tl["btm"][:], AF.Exp, 0.0, 1.0, reads=("btm",), writes=("ebtm",))
        vop("tensor_scalar_mul", tl["negebtm"][:], tl["ebtm"][:], -1.0,
            reads=("ebtm",), writes=("negebtm",))
        sact(tl["ebcmbfm"][:], tl["brows"][:], AF.Exp, tl["brows"][:, 127:128], -1.0,
             reads=("brows",), writes=("ebcmbfm",))
        tpose(pb[5][0:128, 32:48], tl["ebcmbfm"][:], idf16,
              reads=("ebcmbfm", "id_f"), writes=(PB[5],))
        vop("tensor_copy", tl["ebcmbtm"][:], pb[5][0:128, 32:48],
            reads=(PB[5],), writes=("ebcmbtm",))
        sact(tl["ebqrowb"][:], tl["brows"][:], AF.Exp, tl["ln8"][:], 1.0,
             reads=("brows", "ln8"), writes=("ebqrowb",))
        tpose(pb[5][0:1, 48:64], tl["brows"][:, 127:128], idf16,
              reads=("brows", "id_f"), writes=(PB[5],))
        sact(tl["ebcrow"][:], pb[5][0:1, 48:64], AF.Exp, 0.0, 1.0,
             reads=(PB[5],), writes=("ebcrow",))
        vop("tensor_copy", tl["ebcrowb"][:, 0:16], tl["ebcrow"][:],
            reads=("ebcrow",), writes=("ebcrowb",))
        vop("tensor_sub", tl["bcrow"][:], tl["ebcrow"][:], tl["ebcrowb"][:, 0:16],
            reads=("ebcrow", "ebcrowb"), writes=("bcrow",))
        vop("tensor_copy", tl["ebcrowb"][:, 16:32], tl["bcrow"][:],
            reads=("bcrow",), writes=("ebcrowb",))
        mm(pb[5][0:128, 64:80], tl["onesrow"][:], tl["ebcrowb"][:, 0:16], True, False,
           reads=("ebcrowb", "onesrow"), writes=(PB[5],))
        mm(pb[5][0:128, 64:80], tl["onesrow"][:], tl["ebcrowb"][:, 16:32], False, True,
           reads=("ebcrowb", "onesrow"), writes=(PB[5],))
        vop("tensor_copy", tl["ebc"][:], pb[5][0:128, 64:80], reads=(PB[5],), writes=("ebc",))

        for blk in range(NPAIR // 2 if DBG_PAIRS is None else DBG_PAIRS):
            emit_pair_phases(t, cc, blk)

    def emit_pair_phases(t, cc, blk):
        col0 = cc * C
        P2 = [2 * blk, 2 * blk + 1]

        def ctxof(p):
            q = p % 2
            return dict(
                p=p, q=q, b=64 * q,
                qrows=mxc(p // 2, 3 + col0, 3 + col0 + C)[64 * q:64 * q + 64, :],
                krows=mxc(4 + p // 2, 3 + col0, 3 + col0 + C)[64 * q:64 * q + 64, :],
                vpair=mxc(8 + p, 3 + col0, 3 + col0 + C),
                pm=pb[4 + q], PM=PB[4 + q], pq_=pq[q], PQ_=PQ[q],
            )

        CX = [ctxof(p) for p in P2]

        def sl(name, q, w):
            return tl[name][:, q * w:(q + 1) * w], name + str(q)

        def evac(q, out_ap, in_ap, reads, writes):
            if q == 0:
                vop("tensor_copy", out_ap, in_ap, reads=reads, writes=writes)
            else:
                scopy(out_ap, in_ap, reads=reads, writes=writes)

        sch.cur_note = "kkmm"
        for c in CX:
            mm(c["pm"][:, 0:128], c["krows"], c["krows"], True, True,
               reads=("mixed",), writes=(c["PM"],))
            mm(c["pm"][:, 128:256], c["qrows"], c["krows"], True, True,
               reads=("mixed",), writes=(c["PM"],))
        for c in CX:
            q = c["q"]
            kkz, KKZ = sl("KKz", q, 256)
            kq, KQk = sl("KQ", q, 128)
            for hh in range(2):
                h = 2 * c["p"] + hh
                vop("tensor_scalar", kkz[:, hh * 128:(hh + 1) * 128], c["pm"][:, 0:128],
                    tl["betatm"][:, h:h + 1], -1.0, ALU.mult, ALU.mult,
                    reads=(c["PM"], "betatm"), writes=(KKZ,))
            sact(kq, c["pm"][:, 128:256], AF.Identity, 0.0, 0.125,
                 reads=(c["PM"],), writes=(KQk,))
            for hh in range(2):
                vop("tensor_mul", kkz[:, hh * 128:(hh + 1) * 128],
                    kkz[:, hh * 128:(hh + 1) * 128], tl["offdiag"][:],
                    reads=(KKZ, "offdiag"), writes=(KKZ,))
        sch.cur_note = "ebuild"
        for c in CX:
            for hh in range(2):
                h = 2 * c["p"] + hh
                reg = pb[0][:, (2 * c["q"] + hh) * 128:(2 * c["q"] + hh + 1) * 128]
                for s_i in range(3):
                    mm(reg, tl["hsel"][:, h * 128:(h + 1) * 128],
                       tl["bsp"][:, s_i * 128:(s_i + 1) * 128], s_i == 0, False,
                       reads=("hsel", "bsp"), writes=(PB[0],))
                mm(reg, tl["id_b"][:], tl["penL"][:], False, True,
                   reads=("id_b", "penL"), writes=(PB[0],))
        for c in CX:
            e, Ek = sl("E", c["q"], 256)
            for hh in range(2):
                h = 2 * c["p"] + hh
                reg = pb[0][:, (2 * c["q"] + hh) * 128:(2 * c["q"] + hh + 1) * 128]
                sact(e[:, hh * 128:(hh + 1) * 128], reg, AF.Exp,
                     tl["btm"][:, h:h + 1], -1.0, reads=(PB[0], "btm"), writes=(Ek,))
        for c in CX:
            q = c["q"]
            e, Ek = sl("E", q, 256)
            kkz, KKZ = sl("KKz", q, 256)
            kq, KQk = sl("KQ", q, 128)
            n0, N0k = sl("N0", q, 256)
            aa, Ak = sl("A", q, 256)
            for hh in range(2):
                vop("tensor_mul", n0[:, hh * 128:(hh + 1) * 128],
                    kkz[:, hh * 128:(hh + 1) * 128], e[:, hh * 128:(hh + 1) * 128],
                    reads=(KKZ, Ek), writes=(N0k,))
                vop("tensor_mul", aa[:, hh * 128:(hh + 1) * 128],
                    kq, e[:, hh * 128:(hh + 1) * 128],
                    reads=(KQk, Ek), writes=(Ak,))
        sch.cur_note = "tpose"
        for c in CX:
            q = c["q"]
            n0, N0k = sl("N0", q, 256)
            aa, Ak = sl("A", q, 256)
            for hh in range(2):
                tpose(c["pq_"][:, hh * 128:(hh + 1) * 128], n0[:, hh * 128:(hh + 1) * 128],
                      tl["id_b"][:], reads=(N0k, "id_b"), writes=(c["PQ_"],))
                tpose(c["pq_"][:, 256 + hh * 128:256 + (hh + 1) * 128],
                      aa[:, hh * 128:(hh + 1) * 128], tl["id_b"][:],
                      reads=(Ak, "id_b"), writes=(c["PQ_"],))
            tpose(c["pq_"][:, 512:640], c["vpair"], tl["id_b"][:],
                  reads=("mixed", "id_b"), writes=(c["PQ_"],))
            tpose(c["pq_"][:, 640:704], c["krows"],
                  tl["id_b"][c["b"]:c["b"] + 64, c["b"]:c["b"] + 64],
                  reads=("mixed", "id_b"), writes=(c["PQ_"],))
        for c in CX:
            q = c["q"]
            nt, Ntk = sl("Nt", q, 256)
            at, Atk = sl("At", q, 256)
            vt, Vk = sl("Vtm", q, 128)
            kt, Kk = sl("Ktm", q, 64)
            evac(q, nt, c["pq_"][:, 0:256], reads=(c["PQ_"],), writes=(Ntk,))
            evac(q, at, c["pq_"][:, 256:512], reads=(c["PQ_"],), writes=(Atk,))
            evac(q, vt, c["pq_"][:, 512:640], reads=(c["PQ_"],), writes=(Vk,))
            evac(q, kt, c["pq_"][:, 640:704], reads=(c["PQ_"],), writes=(Kk,))
        sch.cur_note = "corr"
        for c in CX:
            q = c["q"]
            b = c["b"]
            for hh in range(2):
                h = 2 * c["p"] + hh
                mm(c["pm"][b:b + 64, 256 + hh * 128:256 + (hh + 1) * 128],
                   tl["hsel"][:, h * 128:h * 128 + 64], tl["ebqrowb"][:], True, True,
                   reads=("hsel", "ebqrowb"), writes=(c["PM"],), tp=(0, b))
            for hh in range(2):
                mm(c["pm"][:, hh * 64:(hh + 1) * 64], c["krows"],
                   tl["Sb"][b:b + 64, c["p"] * 128 + hh * 64: c["p"] * 128 + (hh + 1) * 64],
                   True, True, reads=("mixed", "Sb" + str(q)), writes=(c["PM"],))
        for c in CX:
            q = c["q"]
            b = c["b"]
            rp, Rk = tl["rpre"], "rpre"
            u, Uk = sl("U", q, 128)
            vt, Vk = sl("Vtm", q, 128)
            for hh in range(2):
                h = 2 * c["p"] + hh
                vop("tensor_mul", tl["Qt"][b:b + 64, hh * 128:(hh + 1) * 128], c["qrows"],
                    c["pm"][b:b + 64, 256 + hh * 128:256 + (hh + 1) * 128],
                    reads=("mixed", c["PM"]), writes=("Qt" + str(q),))
            for hh in range(2):
                h = 2 * c["p"] + hh
                vop("scalar_tensor_tensor", rp[:, hh * 64:(hh + 1) * 64],
                    c["pm"][:, hh * 64:(hh + 1) * 64], tl["negebtm"][:, h:h + 1],
                    vt[:, hh * 64:(hh + 1) * 64], ALU.mult, ALU.add,
                    reads=(c["PM"], "negebtm", Vk), writes=(Rk,))
                vop("tensor_scalar_mul", u[:, hh * 64:(hh + 1) * 64],
                    rp[:, hh * 64:(hh + 1) * 64], tl["betatm"][:, h:h + 1],
                    reads=(Rk, "betatm"), writes=(Uk,))
        sch.cur_note = "sq"
        papp = [pb[0], pb[1]]
        PAPP = [PB[0], PB[1]]
        for c in CX:
            q = c["q"]
            u, Uk = sl("U", q, 128)
            nt, Ntk = sl("Nt", q, 256)
            for hh in range(2):
                mm(papp[0][:, q * 128 + hh * 64:q * 128 + (hh + 1) * 64],
                   nt[:, hh * 128:(hh + 1) * 128], u[:, hh * 64:(hh + 1) * 64],
                   True, True, reads=(Ntk, Uk), writes=(PAPP[0],))
        for c in CX:
            q = c["q"]
            u, Uk = sl("U", q, 128)
            vop("tensor_add", u, u, papp[0][:, q * 128:(q + 1) * 128],
                reads=(Uk, PAPP[0]), writes=(Uk,))
        sqbank = {0: [pb[2], pb[3]], 1: [pb[4], pb[5]]}
        SQBANK = {0: [PB[2], PB[3]], 1: [PB[4], PB[5]]}
        for lv in range(1, 7):
            for c in CX:
                q = c["q"]
                bank = sqbank[q][lv % 2]
                bkey = SQBANK[q][lv % 2]
                n0, N0k = sl("N0", q, 256)
                nt, Ntk = sl("Nt", q, 256)
                if lv == 1:
                    lo = [(n0, N0k, slice(0, 128)), (n0, N0k, slice(128, 256))]
                    up = [(nt, Ntk, slice(0, 128)), (nt, Ntk, slice(128, 256))]
                else:
                    pt, pk = sl("sqA" if (lv - 1) % 2 else "sqB", q, 512)
                    lo = [(pt, pk, slice(0, 128)), (pt, pk, slice(256, 384))]
                    up = [(pt, pk, slice(128, 256)), (pt, pk, slice(384, 512))]
                for hh in range(2):
                    lt, lk, ls = lo[hh]
                    ut, uk, us = up[hh]
                    if lv < 6:
                        mm(bank[:, hh * 256:hh * 256 + 128], ut[:, us], lt[:, ls],
                           True, True, reads=(uk, lk), writes=(bkey,))
                    mm(bank[:, hh * 256 + 128:hh * 256 + 256], lt[:, ls], ut[:, us],
                       True, True, reads=(lk, uk), writes=(bkey,))
            for c in CX:
                q = c["q"]
                bank = sqbank[q][lv % 2]
                bkey = SQBANK[q][lv % 2]
                sq_t, sq_k = sl("sqA" if lv % 2 else "sqB", q, 512)
                if lv < 6:
                    evac(q, sq_t, bank[:], reads=(bkey,), writes=(sq_k,))
                else:
                    evac(q, sq_t[:, 128:256], bank[:, 128:256], reads=(bkey,), writes=(sq_k,))
                    evac(q, sq_t[:, 384:512], bank[:, 384:512], reads=(bkey,), writes=(sq_k,))
            for c in CX:
                q = c["q"]
                u, Uk = sl("U", q, 128)
                sq_t, sq_k = sl("sqA" if lv % 2 else "sqB", q, 512)
                for hh in range(2):
                    us = slice(128, 256) if hh == 0 else slice(384, 512)
                    mm(papp[lv % 2][:, q * 128 + hh * 64:q * 128 + (hh + 1) * 64],
                       sq_t[:, us], u[:, hh * 64:(hh + 1) * 64],
                       True, True, reads=(sq_k, Uk), writes=(PAPP[lv % 2],))
            for c in CX:
                q = c["q"]
                u, Uk = sl("U", q, 128)
                vop("tensor_add", u, u, papp[lv % 2][:, q * 128:(q + 1) * 128],
                    reads=(Uk, PAPP[lv % 2]), writes=(Uk,))
        sch.cur_note = "Omm"
        for c in CX:
            q = c["q"]
            b = c["b"]
            u, Uk = sl("U", q, 128)
            at, Atk = sl("At", q, 256)
            for hh in range(2):
                mm(c["pm"][hh * 64:(hh + 1) * 64, 256:384], u[:, hh * 64:(hh + 1) * 64],
                   at[:, hh * 128:(hh + 1) * 128], True, False,
                   reads=(Uk, Atk), writes=(c["PM"],), tp=(0, hh * 64))
                mm(c["pm"][hh * 64:(hh + 1) * 64, 256:384],
                   tl["Sb"][b:b + 64, c["p"] * 128 + hh * 64: c["p"] * 128 + (hh + 1) * 64],
                   tl["Qt"][b:b + 64, hh * 128:(hh + 1) * 128],
                   False, True, reads=("Sb" + str(q), "Qt" + str(q)), writes=(c["PM"],),
                   tp=(b, hh * 64))
        for c in CX:
            q = c["q"]
            hp, Hk = sl("hpre", q, 128)
            vop("tensor_mul", hp, c["pm"][:, 256:384],
                tl["zs"][:, c["p"] * TT + col0: c["p"] * TT + col0 + C],
                reads=(c["PM"], "zs"), writes=(Hk,))
        sch.cur_note = "rms"
        for c in CX:
            q = c["q"]
            hp, Hk = sl("hpre", q, 128)
            hq, HQk = sl("hsq", q, 128)
            sch.op("s", (lambda eng, o=hq, i=hp:
                         eng.activation(o, i, AF.Square, bias=0.0, scale=1.0)),
                   reads=(Hk,), writes=(HQk,))
        for c in CX:
            q = c["q"]
            hq, HQk = sl("hsq", q, 128)
            mm(c["pm"][0:2, 384:512], tl["blockhalf"][:], hq, True, True,
               reads=("blockhalf", HQk), writes=(c["PM"],))
        for c in CX:
            q = c["q"]
            istd, Ik = sl("istd", q, 128)
            sact(istd, c["pm"][0:2, 384:512], AF.Ln, tl["eps_rms"][:], 1.0 / 64,
                 reads=(c["PM"], "eps_rms"), writes=(Ik,))
            sact(istd, istd, AF.Exp, 0.0, -0.5, reads=(Ik,), writes=(Ik,))
        for c in CX:
            q = c["q"]
            istd, Ik = sl("istd", q, 128)
            istdb, Ibk = sl("istdb", q, 128)
            vop("tensor_copy", istdb, istd, reads=(Ik,), writes=(Ibk,))
        for c in CX:
            q = c["q"]
            istdb, Ibk = sl("istdb", q, 128)
            mm(c["pm"][:, 384:512], tl["halfind"][:], istdb, True, True,
               reads=("halfind", Ibk), writes=(c["PM"],))
        for c in CX:
            q = c["q"]
            hp, Hk = sl("hpre", q, 128)
            vop("scalar_tensor_tensor", tl["hb"][:, c["p"] * TT + col0: c["p"] * TT + col0 + C],
                hp, tl["nwdup"][:], c["pm"][:, 384:512], ALU.mult, ALU.mult,
                reads=(Hk, "nwdup", c["PM"]), writes=("hb",))
        sch.cur_note = "state"
        for c in CX:
            q = c["q"]
            u, Uk = sl("U", q, 128)
            for hh in range(2):
                h = 2 * c["p"] + hh
                vop("tensor_scalar_mul", u[:, hh * 64:(hh + 1) * 64],
                    u[:, hh * 64:(hh + 1) * 64], tl["ebcmbtm"][:, h:h + 1],
                    reads=(Uk, "ebcmbtm"), writes=(Uk,))
        for c in CX:
            q = c["q"]
            b = c["b"]
            u, Uk = sl("U", q, 128)
            kt, Kk = sl("Ktm", q, 64)
            for hh in range(2):
                mm(c["pm"][b:b + 64, hh * 64:(hh + 1) * 64], kt,
                   u[:, hh * 64:(hh + 1) * 64], True, True,
                   reads=(Kk, Uk), writes=(c["PM"],), tp=(0, b))
        for c in CX:
            q = c["q"]
            b = c["b"]
            for hh in range(2):
                h = 2 * c["p"] + hh
                col = c["p"] * 128 + hh * 64
                vop("scalar_tensor_tensor", tl["S"][b:b + 64, col:col + 64],
                    tl["S"][b:b + 64, col:col + 64], tl["ebc"][b:b + 64, h:h + 1],
                    c["pm"][b:b + 64, hh * 64:(hh + 1) * 64], ALU.mult, ALU.add,
                    reads=("S" + str(q), "ebc", c["PM"]), writes=("S" + str(q),))
                evac(q, tl["Sb"][b:b + 64, col:col + 64], tl["S"][b:b + 64, col:col + 64],
                     reads=("S" + str(q),), writes=("Sb" + str(q),))

    def layer_norm(dest):
        sch.cur_note = "ln"
        for kc in range(8):
            vop("tensor_copy", tl["xb"][:, kc * TT:(kc + 1) * TT],
                tl["x32"][:, kc * TT:(kc + 1) * TT], reads=("x32",), writes=("xb",))
        for kc in range(8):
            sch.op("s", (lambda eng, o=tl["hb"][:, kc * TT:(kc + 1) * TT],
                         i=tl["xb"][:, kc * TT:(kc + 1) * TT]:
                         eng.activation(o, i, AF.Square, bias=0.0, scale=1.0)),
                   reads=("xb",), writes=("hb",))
        for kc in range(8):
            mm(pb[2][0:1, 0:TT], tl["onescol"][:], tl["xb"][:, kc * TT:(kc + 1) * TT],
               kc == 0, kc == 7, reads=("xb", "onescol"), writes=(PB[2],))
        for kc in range(8):
            mm(pb[2][0:1, 256:256 + TT], tl["onescol"][:], tl["hb"][:, kc * TT:(kc + 1) * TT],
               kc == 0, kc == 7, reads=("hb", "onescol"), writes=(PB[2],))
        vop("tensor_scalar_mul", tl["lnrow"][:, 0:TT], pb[2][0:1, 0:TT], 1.0 / D,
            reads=(PB[2],), writes=("lnrow",))
        vop("tensor_scalar_mul", tl["lnrow"][:, TT:2 * TT], pb[2][0:1, 256:256 + TT], 1.0 / D,
            reads=(PB[2],), writes=("lnrow",))
        vop("tensor_copy", tl["lnb"][:, 0:TT], tl["lnrow"][:, 0:TT],
            reads=("lnrow",), writes=("lnb",))
        vop("tensor_mul", tl["lnrow"][:, 0:TT], tl["lnrow"][:, 0:TT],
            tl["lnrow"][:, 0:TT], reads=("lnrow",), writes=("lnrow",))
        vop("tensor_sub", tl["lnrow"][:, TT:2 * TT], tl["lnrow"][:, TT:2 * TT],
            tl["lnrow"][:, 0:TT], reads=("lnrow",), writes=("lnrow",))
        sact(tl["lnrow"][:, TT:2 * TT], tl["lnrow"][:, TT:2 * TT], AF.Ln,
             tl["eps_ln"][:], 1.0, reads=("lnrow", "eps_ln"), writes=("lnrow",))
        sact(tl["lnrow"][:, TT:2 * TT], tl["lnrow"][:, TT:2 * TT], AF.Exp,
             0.0, -0.5, reads=("lnrow",), writes=("lnrow",))
        vop("tensor_copy", tl["lnb"][:, TT:2 * TT], tl["lnrow"][:, TT:2 * TT],
            reads=("lnrow",), writes=("lnb",))

        mm(pb[3][:, 0:TT], tl["onesrow"][:], tl["lnb"][:, 0:TT], True, True,
           reads=("lnb", "onesrow"), writes=(PB[3],))
        mm(pb[3][:, 256:256 + TT], tl["onesrow"][:], tl["lnb"][:, TT:2 * TT], True, True,
           reads=("lnb", "onesrow"), writes=(PB[3],))
        for kc in range(8):
            vop("tensor_sub", tl["ftmp"][:], tl["x32"][:, kc * TT:(kc + 1) * TT],
                pb[3][:, 0:TT], reads=("x32", PB[3]), writes=("ftmp",))
            vop("tensor_mul", tl["x32"][:, kc * TT:(kc + 1) * TT], tl["ftmp"][:],
                pb[3][:, 256:256 + TT], reads=("ftmp", PB[3]), writes=("x32",))
        if dest is not None:
            for kc in range(8):
                scopy(tl[dest][:, kc * TT:(kc + 1) * TT], tl["x32"][:, kc * TT:(kc + 1) * TT],
                      reads=("x32",), writes=("hb" if dest == "xq" else dest,))

    def emit_tile(t):
        x_sl = dr["xT"][:, t * TT:(t + 1) * TT].rearrange("(c p) j -> p c j", p=128)
        dma("x", tl["x32"][:].rearrange("p (c j) -> p c j", c=8), x_sl,
            reads=("x32",), writes=("x32",))
        for kc in range(8):
            vop("tensor_copy", tl["xb"][:, kc * TT:(kc + 1) * TT],
                tl["x32"][:, kc * TT:(kc + 1) * TT], reads=("x32",), writes=("xb",))

        if DBG_STAGE == "prol":
            return

        def dense(oc, nkc, wt, wkey, stride, bank):
            for kc in range(nkc):
                mm(pb[bank][:, 0:TT], tl[wt][:, kc * stride + oc * 128: kc * stride + (oc + 1) * 128],
                   tl["xb" if wt in ("wqkv", "wz", "wba") else
                      ("hb" if wt == "wout" else ("xq" if wt == "w1" else "h1"))]
                   [:, kc * TT:(kc + 1) * TT],
                   kc == 0, kc == nkc - 1,
                   reads=(wkey, "xb" if wt in ("wqkv", "wz", "wba") else
                          ("hb" if wt in ("wout", "w1") else "h1")),
                   writes=(PB[bank],))

        sch.cur_note = "qkvproj"
        for oc in range(16):
            bank = oc % 2
            dense(oc, 8, "wqkv", "wqkv", CONV, bank)
            scopy(mxc(oc, 3, 3 + TT), pb[bank][:, 0:TT], reads=(PB[bank],), writes=("mixed",))
        sch.cur_note = "zproj"
        for oc in range(8):
            bank = oc % 2
            dense(oc, 8, "wz", "wz", VAL, bank)
            sact(tl["zs"][:, oc * TT:(oc + 1) * TT], pb[bank][:, 0:TT], AF.Silu, 0.0, 1.0,
                 reads=(PB[bank],), writes=("zs",))
        sch.cur_note = "baproj"
        for kc in range(8):
            mm(pb[0][0:48, 0:TT], tl["wba"][:, kc * 48:(kc + 1) * 48],
               tl["xb"][:, kc * TT:(kc + 1) * TT], kc == 0, kc == 7,
               reads=("xb", "wba"), writes=(PB[0],))
        vop("tensor_copy", tl["bat"][:, 0:TT], pb[0][0:48, 0:TT],
            reads=(PB[0],), writes=("bat",))

        if DBG_STAGE == "proj":
            return
        sch.cur_note = "conv"
        for c in range(16):
            cw = lambda j: tl["convw"][:, c * 4 + j: c * 4 + j + 1]
            vop("tensor_scalar_mul", tl["cvt"][:], mxc(c, 0, TT), cw(0),
                reads=("mixed", "convw"), writes=("cvt",))
            for j in (1, 2, 3):
                vop("scalar_tensor_tensor", tl["cvt"][:], mxc(c, j, j + TT), cw(j),
                    tl["cvt"][:], ALU.mult, ALU.add,
                    reads=("mixed", "convw", "cvt"), writes=("cvt",))
            vop("tensor_copy", mxc(c, 0, 3), mxc(c, TT, TT + 3),
                reads=("mixed",), writes=("mixed",))
            sact(mxc(c, 3, 3 + TT), tl["cvt"][:], AF.Silu, 0.0, 1.0,
                 reads=("cvt",), writes=("mixed",))

        if DBG_STAGE == "conv":
            return
        sact(tl["gsc"][:, 0:TT], tl["bat"][0:16, 0:TT], AF.Exp, tl["dtb16"][:], 1.0,
             reads=("bat", "dtb16"), writes=("gsc",))
        vop("tensor_scalar_add", tl["gsc"][:, 0:TT], tl["gsc"][:, 0:TT], 1.0,
            reads=("gsc",), writes=("gsc",))
        sact(tl["ga"][:, 0:TT], tl["gsc"][:, 0:TT], AF.Ln, 0.0, 1.0,
             reads=("gsc",), writes=("ga",))
        vop("tensor_scalar_mul", tl["ga"][:, 0:TT], tl["ga"][:, 0:TT], tl["negA"][:],
            reads=("ga", "negA"), writes=("ga",))

        if DBG_STAGE == "ga":
            return
        for cc in range(NCH if DBG_CHUNKS is None else DBG_CHUNKS):
            emit_chunk(t, cc)
        if DBG_SCAN_ONLY or DBG_STAGE == "scan":
            return

        sch.cur_note = "woutproj"
        for oc in range(8):
            bank = oc % 2
            dense(oc, 8, "wout", "wout", D, bank)
            vop("tensor_add", tl["x32"][:, oc * TT:(oc + 1) * TT], pb[bank][:, 0:TT],
                tl["x32"][:, oc * TT:(oc + 1) * TT], reads=(PB[bank], "x32"), writes=("x32",))
        layer_norm("xq")
        if DBG_STOP_LN1:
            return

        sch.cur_note = "ffn1"
        for oc1 in range(32):
            bank = oc1 % 2
            dense(oc1, 8, "w1", "w1", H1, bank)
            sact(tl["f16t"][:], pb[bank][:, 0:TT], AF.Relu, tl["b1q"][:, oc1:oc1 + 1], 1.0,
                 reads=(PB[bank], "b1q"), writes=("f16t",))
            vop("tensor_copy", tl["h1"][:, oc1 * TT:(oc1 + 1) * TT], tl["f16t"][:],
                reads=("f16t",), writes=("h1",))
        sch.cur_note = "ffn2"
        for oc2 in range(8):
            bank = oc2 % 2
            dense(oc2, 32, "w2", "w2", D, bank)
            sact(tl["f16t"][:], pb[bank][:, 0:TT], AF.Identity, tl["b2q"][:, oc2:oc2 + 1], 1.0,
                 reads=(PB[bank], "b2q"), writes=("f16t",))
            if oc2 % 4 == 0:
                mh_sl = dr["maskT"][oc2 * 128:(oc2 + 4) * 128, t * TT:(t + 1) * TT] \
                    .rearrange("(c p) j -> p c j", p=128)
                dma("m", tl["msk"][:].rearrange("p (c j) -> p c j", c=4), mh_sl,
                    reads=("msk",), writes=("msk",), queue="g")
            vop("tensor_mul", tl["ftmp"][:], tl["f16t"][:], tl["msk"][:, (oc2 % 4) * TT:(oc2 % 4 + 1) * TT],
                reads=("f16t", "msk"), writes=("ftmp",))
            vop("tensor_add", tl["x32"][:, oc2 * TT:(oc2 + 1) * TT], tl["ftmp"][:],
                tl["x32"][:, oc2 * TT:(oc2 + 1) * TT], reads=("ftmp", "x32"), writes=("x32",))
        layer_norm(None)
        o_sl = dr["outT"][:, t * TT:(t + 1) * TT].rearrange("(c p) j -> p c j", p=128)
        dma("o", o_sl, tl["x32"][:].rearrange("p (c j) -> p c j", c=8),
            reads=("x32",), writes=())

    for t in range(N_TILES_EMIT):
        emit_tile(t)

    sch.finalize()
    global _LAST_SCHED
    _LAST_SCHED = sch
    with ctx:
        sch.emit(nc, ctx)
    return nc


_LAST_SCHED = None


_NC_CACHE = {}


def _build_cached():
    key = (N_TILES_EMIT, DEBUG_TAPS)
    if key not in _NC_CACHE:
        _NC_CACHE[key] = build_program()
    return _NC_CACHE[key]


def kernel(L0_attn_hidden_states, L0_attn_W_qkv, L0_attn_conv_w, L0_attn_W_z,
           L0_attn_W_b, L0_attn_W_a, L0_attn_dt_bias, L0_attn_A_log,
           L0_attn_norm_weight, L0_attn_W_out, L0_ffn_w1, L0_ffn_b1,
           L0_ffn_w2, L0_ffn_b2):
    import jax

    x = np.asarray(L0_attn_hidden_states, dtype=np.float32)
    wqkv = np.asarray(L0_attn_W_qkv, dtype=np.float32)
    convw = np.asarray(L0_attn_conv_w, dtype=np.float32)
    wz = np.asarray(L0_attn_W_z, dtype=np.float32)
    wb = np.asarray(L0_attn_W_b, dtype=np.float32)
    wa = np.asarray(L0_attn_W_a, dtype=np.float32)
    dtb = np.asarray(L0_attn_dt_bias, dtype=np.float32)
    alog = np.asarray(L0_attn_A_log, dtype=np.float32)
    nw = np.asarray(L0_attn_norm_weight, dtype=np.float32)
    wout = np.asarray(L0_attn_W_out, dtype=np.float32)
    w1 = np.asarray(L0_ffn_w1, dtype=np.float32)
    b1 = np.asarray(L0_ffn_b1, dtype=np.float32)
    w2 = np.asarray(L0_ffn_w2, dtype=np.float32)
    b2 = np.asarray(L0_ffn_b2, dtype=np.float32)

    with jax.default_device(jax.devices("cpu")[0]):
        keep = np.asarray(jax.random.bernoulli(jax.random.key(42, impl="threefry2x32"), 0.8, x.shape))
    maskv = np.where(keep, np.float32(1.25), np.float32(0.0))

    wba = np.zeros((48, D), np.float32)
    wba[0:16] = wa
    wba[32:48] = wb
    shared = {
        "wqkvT": np.ascontiguousarray(wqkv.T),
        "wzT": np.ascontiguousarray(wz.T),
        "wbaT": np.ascontiguousarray(wba.T),
        "woutT": np.ascontiguousarray(wout.T),
        "w1T": np.ascontiguousarray(w1.T),
        "w2T": np.ascontiguousarray(w2.T),
        "convw": np.ascontiguousarray(
            convw[:, 0, :].reshape(16, 128, 4).transpose(1, 0, 2).reshape(128, 64)),
        "b1r": np.ascontiguousarray(b1.reshape(32, 128).T),
        "b2r": np.ascontiguousarray(b2.reshape(8, 128).T),
        "alog16": np.ascontiguousarray(alog.reshape(16, 1)),
        "dtb16": np.ascontiguousarray(dtb.reshape(16, 1)),
        "nwdup": np.ascontiguousarray(np.tile(nw, 2).reshape(128, 1)),
    }
    in_maps = []
    for c in range(NCORE):
        m = dict(shared)
        m["xT"] = np.ascontiguousarray(x[c].T)
        m["maskT"] = np.ascontiguousarray(maskv[c].T.astype(np.float32))
        in_maps.append(m)

    nc = _build_cached()
    res = run_bass_kernel_spmd(nc, in_maps, core_ids=list(range(NCORE)))
    out = np.empty((NCORE, T, D), dtype=np.float32)
    for c in range(NCORE):
        out[c] = res.results[c]["outT"].T
    return out
